# version 16
# baseline (speedup 1.0000x reference)
import numpy as np
import ml_dtypes
from contextlib import ExitStack
import concourse.bass as bass
import concourse.mybir as mybir
from concourse.bass_utils import run_bass_kernel_spmd
from concourse.bass_core_rust import IndirectOffsetOnAxis

F32 = mybir.dt.float32
BF16 = mybir.dt.bfloat16
AF = mybir.ActivationFunctionType
ALU = mybir.AluOpType
AX = mybir.AxisListType

NCORE = 8
S = 8192
DM = 2048
OWN = 1024
HALO = 128
EXT = OWN + 2 * HALO
NF = DM // 128
NE = 8
INTER = 2816
NPAIR = INTER // 128
GUBLK = 2 * NPAIR
NDM = DM // 128
H, HKV, D = 32, 8, 64
NMQ, NMK = 16, 4
SW = 128
WL = 150000.0
RSCALE = float(D) ** -0.25
NEG = -1.0e9
EPS = 1e-5
NTT = EXT // 128
TS = OWN
QG = 256
NQG = OWN // QG
KW = 384

DMA_Q = ("sy", "gp")
ENGS = ("pe", "ve", "se", "sy", "gp")


def chunk_list(t0, tn, width=512):
    out = []
    o = t0
    while o < t0 + tn:
        w = min(width, t0 + tn - o)
        out.append((o, w))
        o += w
    return out


class Prog:

    def __init__(self):
        self.streams = {e: [] for e in ENGS}
        self.count = {e: 0 for e in ENGS}
        self.mark = {}

    def op(self, eng, fn, wait=(), inc=False):
        waits = []
        for evt in wait:
            if evt is None:
                continue
            p, v = evt
            if p == eng and eng == "pe":
                continue
            k = (eng, p)
            if self.mark.get(k, 0) < v:
                self.mark[k] = v
                waits.append((p, v))
        ev = None
        if inc:
            self.count[eng] += 16 if eng in DMA_Q else 1
            ev = (eng, self.count[eng])
        self.streams[eng].append((tuple(waits), fn, inc))
        return ev

    def barrier(self):
        evs = [(e, self.count[e]) for e in ENGS if self.count[e] > 0]
        for e in ENGS:
            for p, v in evs:
                if p == e:
                    continue
                k = (e, p)
                if self.mark.get(k, 0) < v:
                    self.mark[k] = v
                    self.streams[e].append((((p, v),), None, False))

    def emit(self, eng_obj, eng_name, sems):
        step = 16 if eng_name in DMA_Q else 1
        for waits, fn, inc in self.streams[eng_name]:
            for p, v in waits:
                eng_obj.wait_ge(sems[p], v)
            if fn is None:
                continue
            ins = fn(eng_obj)
            if inc:
                ins.then_inc(sems[eng_name], step)


class RW:

    def __init__(self, name=""):
        self.name = name
        self.wr = None
        self.rds = []

    def rdep(self):
        return [self.wr] if self.wr is not None else []

    def wdep(self):
        d = list(self.rds)
        if self.wr is not None:
            d.append(self.wr)
        return d

    def wrote(self, ev):
        assert ev is not None, f"RW {self.name}: writer must inc"
        self.wr = ev
        self.rds = []

    def read(self, ev):
        assert ev is not None, f"RW {self.name}: reader must inc"
        self.rds.append(ev)


def build_core_program(upto="all", taps=()):
    nc = bass.Bass("TRN2", target_bir_lowering=False)

    d_x = nc.dram_tensor("xT", [DM, TS], F32, kind="ExternalInput").ap()
    d_gu = nc.dram_tensor("wgu", [NE, GUBLK, 128, 2048], F32, kind="ExternalInput").ap()
    d_wd = nc.dram_tensor("wdn", [NE, NDM, 128, 2816], F32, kind="ExternalInput").ap()
    d_wq = nc.dram_tensor("wq", [NMQ, 128, 2048], F32, kind="ExternalInput").ap()
    d_wk = nc.dram_tensor("wk", [NMK, 128, 2048], F32, kind="ExternalInput").ap()
    d_wv = nc.dram_tensor("wv", [DM, 512], F32, kind="ExternalInput").ap()
    d_wo = nc.dram_tensor("wo", [NMQ, 128, 2048], F32, kind="ExternalInput").ap()
    d_gw = nc.dram_tensor("gw", [128, NF * NE], F32, kind="ExternalInput").ap()
    d_bq = nc.dram_tensor("bq", [128, NMQ], F32, kind="ExternalInput").ap()
    d_bk = nc.dram_tensor("bk", [128, NMK], F32, kind="ExternalInput").ap()
    d_bv = nc.dram_tensor("bv", [128, 512], F32, kind="ExternalInput").ap()
    d_bo = nc.dram_tensor("bo", [128, NMQ], F32, kind="ExternalInput").ap()
    d_gb = nc.dram_tensor("gb", [128, NE], F32, kind="ExternalInput").ap()
    d_eb = nc.dram_tensor("eb", [128, NE], F32, kind="ExternalInput").ap()
    d_snk = nc.dram_tensor("snk", [128, H], F32, kind="ExternalInput").ap()
    d_snko = nc.dram_tensor("snko", [128, H], F32, kind="ExternalInput").ap()
    d_snkn = nc.dram_tensor("snkn", [128, H], F32, kind="ExternalInput").ap()
    d_cos = nc.dram_tensor("ctab", [128, TS], F32, kind="ExternalInput").ap()
    d_sin = nc.dram_tensor("stab", [128, TS], F32, kind="ExternalInput").ap()
    d_msk = nc.dram_tensor("msk", [128, 3 * 256], F32, kind="ExternalInput").ap()
    d_idf = nc.dram_tensor("idf", [128, 128], F32, kind="ExternalInput").ap()
    d_swp = nc.dram_tensor("swp", [128, 128], F32, kind="ExternalInput").ap()
    d_sel = nc.dram_tensor("sel", [NE, NE * 128], F32, kind="ExternalInput").ap()
    d_hidx = nc.dram_tensor("hidx", [128, 2], mybir.dt.int32,
                            kind="ExternalInput").ap()
    d_ccin = nc.dram_tensor("cc_in", [2, 128, 1024], BF16, kind="Internal").ap()
    d_ccout = nc.dram_tensor("cc_out", [16, 128, 1024], BF16, kind="Internal").ap()
    d_y = nc.dram_tensor("y", [OWN, DM], F32, kind="ExternalOutput").ap()
    d_taps = {}

    P = Prog()

    with ExitStack() as outer:
        E = outer.enter_context
        sems = {e: E(nc.semaphore(f"sem_{e}")) for e in ENGS}

        x_sb = E(nc.sbuf_tensor("x_sb", [128, NF * TS], F32))
        ones_col = E(nc.sbuf_tensor("ones_col", [128, 1], BF16))
        ones_row_f = E(nc.sbuf_tensor("ones_row_f", [1, 128], F32))
        ones_row_b = E(nc.sbuf_tensor("ones_row_b", [1, 128], BF16))
        eps_t = E(nc.sbuf_tensor("eps_t", [1, 1], F32))
        inv_t = E(nc.sbuf_tensor("inv_t", [1, 1], F32))
        half_t = E(nc.sbuf_tensor("half_t", [128, 1], F32))
        id_f = E(nc.sbuf_tensor("id_f", [128, 128], F32))
        gb_bc = E(nc.sbuf_tensor("gb_bc", [128, NE], F32))
        eb_bc = E(nc.sbuf_tensor("eb_bc", [128, NE], F32))
        gw_bf = E(nc.sbuf_tensor("gw_bf", [128, NF * NE], BF16))
        sel_bf = E(nc.sbuf_tensor("sel_bf", [NE, NE * 128], BF16))

        ev_x0 = P.op("sy", lambda s: s.dma_start(
            out=x_sb.ap().rearrange("p (f t) -> p f t", f=NF),
            in_=d_x.rearrange("(f p) t -> p f t", p=128)), inc=True)
        ev_misc = None
        for dst, src in ((id_f, d_idf), (gb_bc, d_gb), (eb_bc, d_eb)):
            ev_misc = P.op("sy", lambda s, a=dst, b=src:
                           s.dma_start(out=a[:], in_=b), inc=True)
        P.op("gp", lambda g: g.dma_start(out=gw_bf[:], in_=d_gw), inc=True)
        P.op("gp", lambda g: g.dma_start(out=sel_bf[:], in_=d_sel), inc=True)
        for t, vv in ((ones_col, 1.0), (ones_row_f, 1.0), (ones_row_b, 1.0),
                      (eps_t, EPS), (inv_t, 1.0 / DM), (half_t, 0.5)):
            P.op("ve", lambda v, a=t, c=vv: v.memset(a[:], c), inc=True)
        P.barrier()

        def alloc_psum_pool(ctx, tag):
            return [(ctx.enter_context(nc.psum_tensor(f"ps_{tag}_{i}", [128, 512], F32)),
                     RW(f"ps_{tag}_{i}")) for i in range(8)]

        def emit_ln(ctx, tag, lnT, tok0, tokn, pool):
            ev_out = None
            ev_chunks = {}
            with ExitStack() as c2:
                C = c2.enter_context
                xb = [C(nc.sbuf_tensor(f"xb{tag}{i}", [128, 512], BF16)) for i in range(2)]
                sq = [C(nc.sbuf_tensor(f"sq{tag}{i}", [128, 512], BF16)) for i in range(2)]
                row_m = C(nc.sbuf_tensor(f"rm{tag}", [1, 512], F32))
                row_b = C(nc.sbuf_tensor(f"rb{tag}", [1, 512], F32))
                row_c = C(nc.sbuf_tensor(f"rc{tag}", [1, 512], F32))
                ln_tmp = C(nc.sbuf_tensor(f"lt{tag}", [128, 512], F32))
                xbrw = [RW("xb0"), RW("xb1")]
                pst, strw = pool[6]
                pss, ssrw = pool[5]
                pra, prarw = pool[0]
                prb, prbrw = pool[1]
                for (c0, cn) in chunk_list(tok0, tokn):
                    ev_mm = None
                    for k in range(NF):
                        sl = k % 2
                        xcol = x_sb[:, k * TS + c0: k * TS + c0 + cn]
                        evc = P.op("se", lambda s, a=xb[sl], b=xcol, n=cn:
                                   s.copy(a[:, 0:n], b),
                                   wait=xbrw[sl].wdep(), inc=True)
                        evs = P.op("se", lambda s, a=sq[sl], b=xcol, n=cn:
                                   s.activation(a[:, 0:n], b, AF.Square), inc=True)
                        P.op("pe", lambda t, a=xb[sl], n=cn, k=k:
                             t.matmul(pst[0:1, 0:n], ones_col[:], a[:, 0:n],
                                      start=(k == 0), stop=(k == NF - 1)),
                             wait=[evs] + ((strw.wdep() + ssrw.wdep())
                                           if k == 0 else []))
                        ev_mm = P.op("pe", lambda t, a=sq[sl], n=cn, k=k:
                                     t.matmul(pss[0:1, 0:n], ones_col[:], a[:, 0:n],
                                              start=(k == 0), stop=(k == NF - 1)),
                                     inc=True)
                        xbrw[sl].wrote(ev_mm)
                    strw.wrote(ev_mm)
                    ssrw.wrote(ev_mm)
                    ev_q1 = P.op("ve", lambda v, n=cn: v.tensor_scalar_mul(
                        row_m[0:1, 0:n], pst[0:1, 0:n], inv_t[0:1, 0:1]),
                        wait=[ev_mm], inc=True)
                    ev = P.op("ve", lambda v, n=cn: v.tensor_scalar_mul(
                        row_b[0:1, 0:n], pss[0:1, 0:n], inv_t[0:1, 0:1]), inc=True)
                    strw.read(ev)
                    ssrw.read(ev)
                    P.op("ve", lambda v, n=cn: v.tensor_mul(
                        row_c[0:1, 0:n], row_m[0:1, 0:n], row_m[0:1, 0:n]),
                        wait=[ev_q1, ev])
                    ev_var = P.op("ve", lambda v, n=cn: v.tensor_sub(
                        row_b[0:1, 0:n], row_b[0:1, 0:n], row_c[0:1, 0:n]), inc=True)
                    ev_sd = P.op("se", lambda s, n=cn: s.activation(
                        row_b[0:1, 0:n], row_b[0:1, 0:n], AF.Sqrt,
                        bias=eps_t[0:1, 0:1]), wait=[ev_var], inc=True)
                    ev_rcp = P.op("ve", lambda v, n=cn: v.reciprocal(
                        row_b[0:1, 0:n], row_b[0:1, 0:n]), wait=[ev_sd], inc=True)
                    ev_rows = P.op("ve", lambda v, n=cn: v.tensor_mul(
                        row_c[0:1, 0:n], row_m[0:1, 0:n], row_b[0:1, 0:n]),
                        wait=[ev_rcp], inc=True)
                    ev_b1 = P.op("pe", lambda t, n=cn: t.matmul(
                        pra[:, 0:n], ones_row_f[:], row_b[0:1, 0:n],
                        start=True, stop=True),
                        wait=[ev_rows] + prarw.wdep(), inc=True)
                    prarw.wrote(ev_b1)
                    ev_b2 = P.op("pe", lambda t, n=cn: t.matmul(
                        prb[:, 0:n], ones_row_f[:], row_c[0:1, 0:n],
                        start=True, stop=True), wait=prbrw.wdep(), inc=True)
                    prbrw.wrote(ev_b2)
                    ev_ln = None
                    for k in range(NF):
                        P.op("ve", lambda v, k=k, c0=c0, n=cn: v.tensor_mul(
                            ln_tmp[:, 0:n],
                            x_sb[:, k * TS + c0: k * TS + c0 + n], pra[:, 0:n]),
                            wait=[ev_b1, ev_b2] if k == 0 else [])
                        ev_ln = P.op("ve", lambda v, k=k, c0=c0, n=cn: v.tensor_sub(
                            lnT[:, k * TS + c0: k * TS + c0 + n],
                            ln_tmp[:, 0:n], prb[:, 0:n]),
                            inc=(k == NF - 1))
                    prarw.read(ev_ln)
                    prbrw.read(ev_ln)
                    ev_chunks[c0] = ev_ln
                    ev_out = ev_ln
            return ev_out, ev_chunks

        def emit_router(ctx, tag, lnT, comb_em, tok0, tokn, pool, ev_lnT_or):
            ph, phrw = pool[7]
            ev_ce = None
            with ExitStack() as c2:
                C = c2.enter_context
                r_tm = C(nc.sbuf_tensor(f"rtm{tag}", [128, NE], F32))
                r_t2 = C(nc.sbuf_tensor(f"rt2{tag}", [128, NE], F32))
                m8 = C(nc.sbuf_tensor(f"m8{tag}", [128, 8], F32))
                nmr = C(nc.sbuf_tensor(f"nmr{tag}", [128, 1], F32))
                dsr = C(nc.sbuf_tensor(f"dsr{tag}", [128, 1], F32))
                for ti in range(tok0 // 128, (tok0 + tokn) // 128):
                    ev_mm = None
                    for k in range(NF):
                        ev_mm = P.op("pe", lambda t, k=k, ti=ti: t.matmul(
                            ph[:, 0:NE],
                            lnT[:, k * TS + ti * 128: k * TS + (ti + 1) * 128],
                            gw_bf[:, k * NE:(k + 1) * NE],
                            start=(k == 0), stop=(k == NF - 1)),
                            wait=([ev_lnT_or(ti)] + phrw.wdep()) if k == 0 else [],
                            inc=(k == NF - 1))
                    phrw.wrote(ev_mm)
                    ev1 = P.op("ve", lambda v: v.tensor_add(
                        r_tm[:], ph[:, 0:NE], gb_bc[:]), wait=[ev_mm], inc=True)
                    phrw.read(ev1)
                    if f"rlg{tag}" in taps and ti == tok0 // 128:
                        tap(f"rlg{tag}", r_tm, [128, NE], [ev1])
                    ev_nm = P.op("ve", lambda v: v.reduce_max(
                        nmr[:], r_tm[:], axis=AX.X, negate=True),
                        wait=[ev1], inc=True)
                    ev_ex = P.op("se", lambda s: s.activation(
                        r_tm[:], r_tm[:], AF.Exp, bias=nmr[:, 0:1],
                        accum_out=dsr[:]), wait=[ev_nm], inc=True)
                    ev_rc = P.op("ve", lambda v: v.reciprocal(dsr[:], dsr[:]),
                                 wait=[ev_ex], inc=True)
                    if f"rds{tag}" in taps and ti == tok0 // 128:
                        tap(f"rds{tag}", dsr, [128, 1], [ev_rc])
                        tap(f"rex{tag}", r_tm, [128, NE], [ev_rc])
                    ev_sc = P.op("ve", lambda v: v.tensor_scalar_mul(
                        r_tm[:], r_tm[:], dsr[:, 0:1]),
                        wait=[ev_rc], inc=True)
                    if f"rds{tag}" in taps and ti == tok0 // 128:
                        tap(f"rsc{tag}", r_tm, [128, NE], [ev_sc])
                    ev_eb = P.op("ve", lambda v: v.tensor_add(
                        r_tm[:], r_tm[:], eb_bc[:]), wait=[ev_sc], inc=True)
                    if f"rsm{tag}" in taps and ti == tok0 // 128:
                        tap(f"rsm{tag}", r_tm, [128, NE], [ev_eb])
                    ev_m8 = P.op("ve", lambda v: v.max(out=m8[:], in_=r_tm[:]),
                                 wait=[ev_eb], inc=True)
                    ev_st2 = P.op("ve", lambda v: v.scalar_tensor_tensor(
                        r_t2[:], r_tm[:], m8[:, 1:2], r_tm[:],
                        op0=ALU.is_ge, op1=ALU.mult), wait=[ev_m8], inc=True)
                    ev_cb = P.op("ve", lambda v: v.tensor_scalar_mul(
                        r_t2[:], r_t2[:], half_t[:, 0:1]), wait=[ev_st2], inc=True)
                    ev_tp = P.op("pe", lambda t: t.matmul(
                        ph[0:NE, 0:128], r_t2[:], id_f[:], start=True, stop=True),
                        wait=[ev_cb, ev_misc] + phrw.wdep(), inc=True)
                    phrw.wrote(ev_tp)
                    ev_ce = P.op("ve", lambda v, ti=ti: v.tensor_copy(
                        comb_em[0:NE, ti * 128:(ti + 1) * 128], ph[0:NE, 0:128]),
                        wait=[ev_tp], inc=True)
                    phrw.read(ev_ce)
            return ev_ce

        def tap(name, sbuf, shape, wait):
            dd = nc.dram_tensor("tap_" + name, shape, sbuf.dtype,
                                kind="ExternalOutput").ap()
            d_taps[name] = dd
            P.op("sy", lambda s: s.dma_start(out=dd,
                                             in_=sbuf[0:shape[0], 0:shape[1]]),
                 wait=wait, inc=True)

        def emit_ffn(ctx, tag, tok0, tokn, stage="full"):
            pool = alloc_psum_pool(ctx, tag)
            lnT = ctx.enter_context(nc.sbuf_tensor(f"lnT{tag}", [128, NF * TS], BF16))
            comb_em = ctx.enter_context(nc.sbuf_tensor(f"cem{tag}", [NE, TS], BF16))
            wgu = [ctx.enter_context(nc.sbuf_tensor(f"wgu{tag}{i}", [128, 2048], BF16))
                   for i in range(4)]
            wdn = [ctx.enter_context(nc.sbuf_tensor(f"wdn{tag}{i}", [128, 1408], BF16))
                   for i in range(2)]
            inter = ctx.enter_context(
                nc.sbuf_tensor(f"inter{tag}", [128, NPAIR * TS], BF16))
            sil = [ctx.enter_context(nc.sbuf_tensor(f"sil{tag}0", [128, 512], F32))]
            cbc = ctx.enter_context(nc.sbuf_tensor(f"cbc{tag}", [128, TS], BF16))

            ev_lnT, ev_lnc = emit_ln(ctx, tag, lnT, tok0, tokn, pool)

            def lnc(c0):
                ks = [k for k in ev_lnc if k <= c0]
                return ev_lnc[max(ks)] if ks else ev_lnT
            if f"lnT{tag}" in taps:
                tap(f"lnT{tag}", lnT, [128, NF * TS], [ev_lnT])
            if stage == "ln":
                return ev_lnT
            ev_comb = emit_router(ctx, tag, lnT, comb_em, tok0, tokn, pool,
                                  lambda ti: lnc(ti * 128))
            if f"cem{tag}" in taps:
                tap(f"cem{tag}", comb_em, [NE, TS], [ev_comb])
            if stage == "router":
                return ev_comb

            chunks = chunk_list(tok0, tokn)
            nch = len(chunks)
            wgurw = [RW(f"wgu{i}") for i in range(4)]
            wdnrw = [RW(f"wdn{i}") for i in range(2)]
            silrw = [RW("sil0")]
            cbcrw = RW("cbc")
            pc, pcrw = pool[7]

            def gu_dma(e, blk):
                sl = blk % 4
                ev = P.op("gp", lambda g, e=e, blk=blk, sl=sl: g.dma_start(
                    out=wgu[sl][:], in_=d_gu[e, blk]),
                    wait=wgurw[sl].wdep(), inc=True)
                wgurw[sl].wrote(ev)

            def wd_dma(e, hb):
                m, half = hb // 2, hb % 2
                sl = hb % 2
                ev = P.op("gp", lambda g, e=e, m=m, half=half, sl=sl: g.dma_start(
                    out=wdn[sl][:], in_=d_wd[e, m, :, half * 1408:(half + 1) * 1408]),
                    wait=wdnrw[sl].wdep(), inc=True)
                wdnrw[sl].wrote(ev)

            ev_x_last = None
            dn_ctr = 0
            for e in range(NE):
                for (c0, cn) in chunks:
                    ev = P.op("pe", lambda t, e=e, c0=c0, cn=cn: t.matmul(
                        pc[:, 0:cn], sel_bf[:, e * 128:(e + 1) * 128],
                        comb_em[0:NE, c0:c0 + cn],
                        start=True, stop=True),
                        wait=[ev_comb] + pcrw.wdep(), inc=True)
                    pcrw.wrote(ev)
                    ev = P.op("ve", lambda v, c0=c0, cn=cn: v.tensor_copy(
                        cbc[:, c0:c0 + cn], pc[:, 0:cn]),
                        wait=[ev] + (cbcrw.wdep() if c0 == chunks[0][0] else []),
                        inc=True)
                    pcrw.read(ev)
                ev_cbc = ev
                cbcrw.wrote(ev_cbc)
                gu_dma(e, 0)
                gu_dma(e, 1)
                gu_dma(e, 2)
                ev_int_c = {}
                gu_ctr = 0
                for i in range(NPAIR):
                    gsl, usl = (2 * i) % 4, (2 * i + 1) % 4
                    pus, pgs = [], []
                    for (c0, cn) in chunks:
                        pu, purw = pool[gu_ctr % 6]
                        gu_ctr += 1
                        ev_u = None
                        for k in range(NF):
                            ev_u = P.op("pe", lambda t, k=k, usl=usl, c0=c0, cn=cn,
                                        pu=pu: t.matmul(
                                            pu[:, 0:cn],
                                            wgu[usl][:, k * 128:(k + 1) * 128],
                                            lnT[:, k * TS + c0: k * TS + c0 + cn],
                                            start=(k == 0), stop=(k == NF - 1)),
                                        wait=(wgurw[usl].rdep() + purw.wdep()
                                              + [lnc(c0)]) if k == 0 else [],
                                        inc=(k == NF - 1))
                        purw.wrote(ev_u)
                        pus.append((pu, purw, ev_u))
                    wgurw[usl].read(pus[-1][2])
                    for ci, (c0, cn) in enumerate(chunks):
                        pg, pgrw = pool[gu_ctr % 6]
                        gu_ctr += 1
                        ev_g = None
                        for k in range(NF):
                            ev_g = P.op("pe", lambda t, k=k, gsl=gsl, c0=c0, cn=cn,
                                        pg=pg: t.matmul(
                                            pg[:, 0:cn],
                                            wgu[gsl][:, k * 128:(k + 1) * 128],
                                            lnT[:, k * TS + c0: k * TS + c0 + cn],
                                            start=(k == 0), stop=(k == NF - 1)),
                                        wait=(wgurw[gsl].rdep() + pgrw.wdep())
                                        if k == 0 else [],
                                        inc=(k == NF - 1))
                        pgrw.wrote(ev_g)
                        pgs.append((pg, pgrw, ev_g))
                        ssl = 0
                        pu, purw, ev_u = pus[ci]
                        ev_s = P.op("se", lambda s, ssl=ssl, cn=cn, pg=pg:
                                    s.activation(sil[ssl][:, 0:cn], pg[:, 0:cn],
                                                 AF.Silu),
                                    wait=[ev_g] + silrw[ssl].wdep(), inc=True)
                        pgrw.read(ev_s)
                        silrw[ssl].wrote(ev_s)
                        ev_t = P.op("ve", lambda v, ssl=ssl, i=i, c0=c0, cn=cn,
                                    pu=pu: v.tensor_mul(
                                        inter[:, i * TS + c0: i * TS + c0 + cn],
                                        sil[ssl][:, 0:cn], pu[:, 0:cn]),
                                    wait=[ev_s, ev_u], inc=True)
                        purw.read(ev_t)
                        silrw[ssl].read(ev_t)
                        ev_i = P.op("ve", lambda v, i=i, c0=c0, cn=cn: v.tensor_mul(
                            inter[:, i * TS + c0: i * TS + c0 + cn],
                            inter[:, i * TS + c0: i * TS + c0 + cn],
                            cbc[:, c0:c0 + cn]),
                            wait=[ev_cbc], inc=True)
                        cbcrw.read(ev_i)
                        ev_int_c[ci] = ev_i
                    wgurw[gsl].read(pgs[-1][2])
                    if 2 * i + 3 < GUBLK:
                        gu_dma(e, 2 * i + 3)
                    if 2 * i + 4 < GUBLK:
                        gu_dma(e, 2 * i + 4)
                wd_dma(e, 0)
                wd_dma(e, 1)
                for m in range(NDM):
                    bset = [4, 5, 6] if m % 2 == 0 else [0, 1, 2]
                    ev_d = {}
                    for k in range(NPAIR):
                        msl = (2 * m + k // 11) % 2
                        kk = k % 11
                        for ci, (c0, cn) in enumerate(chunks):
                            ev_d[ci] = P.op("pe", lambda t, k=k, kk=kk, msl=msl,
                                            c0=c0, cn=cn, ci=ci, bset=bset: t.matmul(
                                                pool[bset[ci]][0][:, 0:cn],
                                                wdn[msl][:, kk * 128:(kk + 1) * 128],
                                                inter[:, k * TS + c0: k * TS + c0 + cn],
                                                start=(k == 0), stop=(k == NPAIR - 1)),
                                            wait=(wdnrw[msl].rdep()
                                                  + (pool[bset[ci]][1].wdep()
                                                     + [ev_int_c[ci]] if k == 0
                                                     else [])) if (k == 0 or kk == 0)
                                            else [],
                                            inc=(k == NPAIR - 1 or kk == 10))
                        if kk == 10:
                            wdnrw[msl].read(ev_d[len(chunks) - 1])
                            if 2 * m + k // 11 + 2 < 2 * NDM:
                                wd_dma(e, 2 * m + k // 11 + 2)
                    for ci, (c0, cn) in enumerate(chunks):
                        pd, pdrw = pool[bset[ci]]
                        pdrw.wrote(ev_d[ci])
                        ev_x = P.op("ve", lambda v, m=m, c0=c0, cn=cn, pd=pd:
                                    v.tensor_add(
                                        x_sb[:, m * TS + c0: m * TS + c0 + cn],
                                        x_sb[:, m * TS + c0: m * TS + c0 + cn],
                                        pd[:, 0:cn]),
                                    wait=[ev_d[ci]], inc=True)
                        pdrw.read(ev_x)
                        ev_x_last = ev_x
            return ev_x_last

        with ExitStack() as ctx:
            emit_ffn(ctx, "f1", 0, OWN,
                     stage={"ln1": "ln", "router1": "router"}.get(upto, "full"))
            if "x1" in taps:
                P.barrier()
                tap("x1", x_sb, [128, NF * TS], [])
            P.barrier()

        if upto in ("attn", "all"):
            with ExitStack() as ctx:
                pool = alloc_psum_pool(ctx, "at")
                CE = ctx.enter_context
                msk = CE(nc.sbuf_tensor("msk_sb", [128, 3 * 256], F32))
                ctab = CE(nc.sbuf_tensor("ctab_sb", [128, TS], F32))
                stab = CE(nc.sbuf_tensor("stab_sb", [128, TS], F32))
                snk = CE(nc.sbuf_tensor("snk_sb", [128, H], F32))
                snko = CE(nc.sbuf_tensor("snko_sb", [128, H], F32))
                snkn = CE(nc.sbuf_tensor("snkn_sb", [128, H], F32))
                bq_sb = CE(nc.sbuf_tensor("bq_sb", [128, NMQ], F32))
                bk_sb = CE(nc.sbuf_tensor("bk_sb", [128, NMK], F32))
                bv_sb = CE(nc.sbuf_tensor("bv_sb", [128, 512], F32))
                bo_sb = CE(nc.sbuf_tensor("bo_sb", [128, NMQ], F32))
                swp_f = CE(nc.sbuf_tensor("swp_f", [128, 128], F32))
                lnT = CE(nc.sbuf_tensor("lnT2", [128, NF * TS], BF16))
                v_sb = CE(nc.sbuf_tensor("v_sb", [128, NTT * 512], BF16))
                krot = CE(nc.sbuf_tensor("krot", [128, NMK * EXT], BF16))
                qrot = CE(nc.sbuf_tensor("qrot", [128, NMQ * OWN], BF16))
                wblk = [CE(nc.sbuf_tensor(f"wblk{i}", [128, 2048], BF16))
                        for i in range(3)]
                wvb = [CE(nc.sbuf_tensor(f"wvb{i}", [128, 512], BF16))
                       for i in range(2)]
                wblkrw = [RW(f"wblk{i}") for i in range(3)]
                wvbrw = [RW("wvb0"), RW("wvb1")]
                hidx_sb = CE(nc.sbuf_tensor("hidx_sb", [128, 2], mybir.dt.int32))
                halo_sb = CE(nc.sbuf_tensor("halo_sb", [128, 2048], BF16))
                cc_sem = CE(nc.semaphore("sem_cc"))
                sems["cc"] = cc_sem
                ev_ld = None
                for dst, src in ((msk, d_msk), (ctab, d_cos), (stab, d_sin),
                                 (snk, d_snk), (snkn, d_snkn), (snko, d_snko),
                                 (bq_sb, d_bq),
                                 (bk_sb, d_bk), (bv_sb, d_bv), (bo_sb, d_bo),
                                 (swp_f, d_swp), (hidx_sb, d_hidx)):
                    ev_ld = P.op("sy", lambda s, a=dst, b=src: s.dma_start(
                        out=a[:], in_=b), inc=True)
                ev_lnT, ev_lnc2 = emit_ln(ctx, "l2", lnT, 0, OWN, pool)
                if "lnT2" in taps:
                    tap("lnT2", lnT, [128, NF * TS], [ev_lnT])

                with ExitStack() as c2:
                    C2 = c2.enter_context
                    kf = [C2(nc.sbuf_tensor(f"kf{i}", [128, 512], F32))
                          for i in range(2)]
                    kt1 = C2(nc.sbuf_tensor("kt1", [128, 512], F32))
                    kt2 = C2(nc.sbuf_tensor("kt2", [128, 512], F32))
                    kfrw = [RW("kf0"), RW("kf1")]
                    pctr = 0
                    ev_krot = None
                    for mb in range(NMK):
                        sl = mb % 3
                        ev_w = P.op("gp", lambda g, mb=mb, sl=sl: g.dma_start(
                            out=wblk[sl][:], in_=d_wk[mb]),
                            wait=wblkrw[sl].wdep(), inc=True)
                        wblkrw[sl].wrote(ev_w)
                        ev_mk_last = None
                        for (c0, cn) in chunk_list(0, OWN):
                            pp, pprw = pool[2 + pctr % 2]
                            ps, psrw = pool[4 + pctr % 2]
                            fsl = pctr % 2
                            pctr += 1
                            ev_m = None
                            for k in range(NF):
                                ev_m = P.op("pe", lambda t, k=k, sl=sl, c0=c0, cn=cn,
                                            pp=pp: t.matmul(
                                                pp[:, 0:cn],
                                                wblk[sl][:, k * 128:(k + 1) * 128],
                                                lnT[:, k * TS + c0:k * TS + c0 + cn],
                                                start=(k == 0), stop=(k == NF - 1)),
                                            wait=(wblkrw[sl].rdep() + pprw.wdep()
                                                  + [ev_lnc2.get(c0, ev_lnT)])
                                            if k == 0 else [],
                                            inc=(k == NF - 1))
                            pprw.wrote(ev_m)
                            ev_mk_last = ev_m
                            ev_f = P.op("se", lambda s, fsl=fsl, mb=mb, cn=cn, pp=pp:
                                        s.activation(kf[fsl][:, 0:cn], pp[:, 0:cn],
                                                     AF.Identity, bias=bk_sb[:, mb:mb + 1]),
                                        wait=[ev_m, ev_ld] + kfrw[fsl].wdep(), inc=True)
                            pprw.read(ev_f)
                            kfrw[fsl].wrote(ev_f)
                            ev_sw = P.op("pe", lambda t, fsl=fsl, cn=cn, ps=ps:
                                         t.matmul(ps[:, 0:cn], swp_f[:],
                                                  kf[fsl][:, 0:cn],
                                                  start=True, stop=True),
                                         wait=[ev_f] + psrw.wdep(), inc=True)
                            psrw.wrote(ev_sw)
                            P.op("ve", lambda v, c0=c0, cn=cn, ps=ps: v.tensor_mul(
                                kt1[:, 0:cn], stab[:, c0:c0 + cn],
                                ps[:, 0:cn]),
                                wait=[ev_sw, ev_ld])
                            ev_r1 = P.op("ve", lambda v, fsl=fsl, c0=c0, cn=cn:
                                         v.tensor_mul(kt2[:, 0:cn], kf[fsl][:, 0:cn],
                                                      ctab[:, c0:c0 + cn]),
                                         inc=True)
                            psrw.read(ev_r1)
                            kfrw[fsl].read(ev_r1)
                            ev_krot = P.op("ve", lambda v, mb=mb, c0=c0, cn=cn:
                                           v.tensor_add(
                                               krot[:, mb * EXT + HALO + c0:
                                                    mb * EXT + HALO + c0 + cn],
                                               kt1[:, 0:cn], kt2[:, 0:cn]), inc=True)
                        wblkrw[sl].read(ev_mk_last)

                    ev_vdone = None
                    for tp in range(OWN // 256):
                        t0, t1 = 2 * tp, 2 * tp + 1
                        pp0, pp0rw = pool[6]
                        pp1, pp1rw = pool[7]
                        ev_m0 = ev_m1 = None
                        for k in range(NF):
                            sl = k % 2
                            ev_w = P.op("gp", lambda g, k=k, sl=sl: g.dma_start(
                                out=wvb[sl][:], in_=d_wv[k * 128:(k + 1) * 128, :]),
                                wait=wvbrw[sl].wdep(), inc=True)
                            wvbrw[sl].wrote(ev_w)
                            ev_m0 = P.op("pe", lambda t, k=k, sl=sl, t0=t0: t.matmul(
                                pp0[:, 0:512],
                                lnT[:, k * TS + t0 * 128: k * TS + (t0 + 1) * 128],
                                wvb[sl][:], start=(k == 0), stop=(k == NF - 1)),
                                wait=[ev_w] + (pp0rw.wdep() if k == 0 else []),
                                inc=(k == NF - 1))
                            ev_m1 = P.op("pe", lambda t, k=k, sl=sl, t1=t1: t.matmul(
                                pp1[:, 0:512],
                                lnT[:, k * TS + t1 * 128: k * TS + (t1 + 1) * 128],
                                wvb[sl][:], start=(k == 0), stop=(k == NF - 1)),
                                wait=(pp1rw.wdep() if k == 0 else []),
                                inc=True)
                            wvbrw[sl].read(ev_m1)
                        pp0rw.wrote(ev_m0)
                        pp1rw.wrote(ev_m1)
                        ev_v0 = P.op("ve", lambda v, t0=t0, pp0=pp0: v.tensor_add(
                            v_sb[:, (t0 + 1) * 512:(t0 + 2) * 512], bv_sb[:],
                            pp0[:, 0:512]),
                            wait=[ev_m0, ev_ld], inc=True)
                        pp0rw.read(ev_v0)
                        ev_vdone = P.op("ve", lambda v, t1=t1, pp1=pp1: v.tensor_add(
                            v_sb[:, (t1 + 1) * 512:(t1 + 2) * 512], bv_sb[:],
                            pp1[:, 0:512]),
                            wait=[ev_m1], inc=True)
                        pp1rw.read(ev_vdone)
                with ExitStack() as c2:
                    C2 = c2.enter_context
                    qf = [C2(nc.sbuf_tensor(f"qf{i}", [128, 512], F32))
                          for i in range(2)]
                    qt1 = C2(nc.sbuf_tensor("qt1", [128, 512], F32))
                    qt2 = C2(nc.sbuf_tensor("qt2", [128, 512], F32))
                    qfrw = [RW("qf0"), RW("qf1")]
                    ev_qrot = None
                    for m in range(NMQ):
                        wsl = m % 3
                        ev_w = P.op("gp", lambda gg, m=m, wsl=wsl: gg.dma_start(
                            out=wblk[wsl][:], in_=d_wq[m]),
                            wait=wblkrw[wsl].wdep(), inc=True)
                        wblkrw[wsl].wrote(ev_w)
                        ev_m = None
                        for grp in range(2):
                            g0 = grp * 512
                            fsl = grp % 2
                            pp, pprw = pool[grp % 2]
                            ps, psrw = pool[2 + grp % 2]
                            for k in range(NF):
                                ev_m = P.op("pe", lambda t, k=k, wsl=wsl, g0=g0,
                                            pp=pp: t.matmul(
                                                pp[:, 0:512],
                                                wblk[wsl][:, k * 128:(k + 1) * 128],
                                                lnT[:, k * TS + g0:k * TS + g0 + 512],
                                                start=(k == 0), stop=(k == NF - 1)),
                                            wait=(wblkrw[wsl].rdep() + pprw.wdep())
                                            if k == 0 else [],
                                            inc=(k == NF - 1))
                            pprw.wrote(ev_m)
                            ev_f = P.op("se", lambda s, fsl=fsl, m=m, pp=pp:
                                        s.activation(qf[fsl][:], pp[:, 0:512],
                                                     AF.Identity,
                                                     bias=bq_sb[:, m:m + 1]),
                                        wait=[ev_m] + qfrw[fsl].wdep(), inc=True)
                            pprw.read(ev_f)
                            qfrw[fsl].wrote(ev_f)
                            ev_sw = P.op("pe", lambda t, fsl=fsl, ps=ps: t.matmul(
                                ps[:, 0:512], swp_f[:], qf[fsl][:],
                                start=True, stop=True),
                                wait=[ev_f] + psrw.wdep(), inc=True)
                            psrw.wrote(ev_sw)
                            P.op("ve", lambda v, g0=g0, ps=ps: v.tensor_mul(
                                qt1[:], stab[:, g0:g0 + 512], ps[:, 0:512]),
                                wait=[ev_sw])
                            ev_r1 = P.op("ve", lambda v, fsl=fsl, g0=g0: v.tensor_mul(
                                qt2[:], qf[fsl][:], ctab[:, g0:g0 + 512]), inc=True)
                            psrw.read(ev_r1)
                            qfrw[fsl].read(ev_r1)
                            ev_qrot = P.op("ve", lambda v, m=m, g0=g0: v.tensor_add(
                                qrot[:, m * OWN + g0:m * OWN + g0 + 512],
                                qt1[:], qt2[:]), inc=True)
                        wblkrw[wsl].read(ev_m)

                ev_kv_local = max(ev_krot, ev_vdone,
                                  key=lambda e: e[1] if e else 0)
                ev_pk = None
                for edge, tok0 in ((0, 0), (1, OWN - 128)):
                    for mb in range(NMK):
                        ev_pk = P.op("sy", lambda ss, edge=edge, tok0=tok0, mb=mb:
                                     ss.dma_start(
                                         out=d_ccin[edge, :, mb * 128:(mb + 1) * 128],
                                         in_=krot[:, mb * EXT + HALO + tok0:
                                                  mb * EXT + HALO + tok0 + 128]),
                                     wait=[ev_krot], inc=True)
                    ev_pk = P.op("sy", lambda ss, edge=edge, tok0=tok0:
                                 ss.dma_start(
                                     out=d_ccin[edge, :, 512:1024],
                                     in_=v_sb[:, (1 + tok0 // 128) * 512:
                                              (2 + tok0 // 128) * 512]),
                                 wait=[ev_vdone], inc=True)
                P.op("gp", lambda g: g.collective_compute(
                    "AllGather", mybir.AluOpType.bypass,
                    replica_groups=[list(range(NCORE))],
                    ins=[d_ccin.opt()], outs=[d_ccout.opt()]).then_inc(cc_sem),
                    wait=[ev_pk])
                ev_g0 = P.op("gp", lambda g: g.indirect_dma_start(
                    halo_sb[:, 0:1024], None,
                    d_ccout.rearrange("r p c -> (r p) c"),
                    IndirectOffsetOnAxis(hidx_sb[:, 0:1], 0)),
                    wait=[("cc", 1), ev_ld], inc=True)
                ev_g1 = P.op("gp", lambda g: g.indirect_dma_start(
                    halo_sb[:, 1024:2048], None,
                    d_ccout.rearrange("r p c -> (r p) c"),
                    IndirectOffsetOnAxis(hidx_sb[:, 1:2], 0)), inc=True)
                for mb in range(NMK):
                    P.op("ve", lambda v, mb=mb: v.tensor_copy(
                        krot[:, mb * EXT: mb * EXT + 128],
                        halo_sb[:, mb * 128:(mb + 1) * 128]),
                        wait=[ev_g0] if mb == 0 else [])
                    P.op("ve", lambda v, mb=mb: v.tensor_copy(
                        krot[:, mb * EXT + HALO + OWN: (mb + 1) * EXT],
                        halo_sb[:, 1024 + mb * 128: 1024 + (mb + 1) * 128]),
                        wait=[ev_g1] if mb == 0 else [])
                P.op("ve", lambda v: v.tensor_copy(
                    v_sb[:, 0:512], halo_sb[:, 512:1024]))
                ev_halo = P.op("ve", lambda v: v.tensor_copy(
                    v_sb[:, (NTT - 1) * 512: NTT * 512],
                    halo_sb[:, 1536:2048]), inc=True)
                

                if "krot" in taps:
                    tap("krot", krot, [128, NMK * EXT], [ev_krot])
                if "v_sb" in taps:
                    tap("v_sb", v_sb, [128, NTT * 512], [ev_vdone])
                P.barrier()

                with ExitStack() as c2:
                    C2 = c2.enter_context
                    aout = C2(nc.sbuf_tensor("aout", [128, NMQ * QG], BF16))
                    pbuf = [C2(nc.sbuf_tensor(f"pbuf{i}", [128, KW], BF16))
                            for i in range(8)]
                    ptb = [C2(nc.sbuf_tensor(f"ptb{i}", [128, KW], BF16))
                           for i in range(4)]
                    nm4 = C2(nc.sbuf_tensor("anm4", [128, 4], F32))
                    ds4 = C2(nc.sbuf_tensor("ads4", [128, 4], F32))
                    es4 = C2(nc.sbuf_tensor("aes4", [128, 4], F32))
                    diag = [C2(nc.sbuf_tensor(f"adg{i}", [128, 128], BF16))
                            for i in range(8)]
                    pbrw = [RW(f"pb{i}") for i in range(8)]
                    ptrw = [RW(f"pt{i}") for i in range(4)]
                    dgrw = [RW(f"dg{i}") for i in range(8)]
                    for g in (1, 2, 0, 3):
                        g0 = g * QG
                        pav, pavrw = pool[3]
                        ev_ao = None
                        SKEW = 5
                        st_a = {}
                        st_b = {}
                        quad_prev = [None, None]
                        exp_prev = [None] * 4

                        def stage_a(idx):
                            qq, hh = idx // H, idx % H
                            qt = 2 * g + qq
                            mvar = 1 if qt == 0 else (2 if qt == NTT - 3 else 0)
                            ph, sub = hh // 2, hh % 2
                            gg_, i_ = ph // 4, ph % 4
                            h = gg_ * 8 + sub * 4 + i_
                            kvg = 2 * gg_ + sub
                            rbase = sub * 64
                            sl = idx % 4
                            slb = idx % 8
                            psc, pscrw = pool[4 + sl]
                            ev_sc = P.op("pe", lambda t, ph=ph, rbase=rbase,
                                         kvg=kvg, qt=qt, psc=psc: t.matmul(
                                             psc[:, 0:KW],
                                             qrot[rbase:rbase + 64,
                                                  ph * OWN + qt * 128:
                                                  ph * OWN + qt * 128 + 128],
                                             krot[rbase:rbase + 64,
                                                  (kvg // 2) * EXT + qt * 128:
                                                  (kvg // 2) * EXT + qt * 128 + KW],
                                             start=True, stop=True),
                                         wait=[ev_qrot,
                                               ev_halo if qt in (0, NTT - 3)
                                               else ev_kv_local]
                                         + pscrw.wdep(),
                                         inc=True)
                            pscrw.wrote(ev_sc)
                            ev_mk = P.op("ve", lambda v, mvar=mvar, psc=psc:
                                         v.tensor_tensor(
                                             psc.ap().rearrange(
                                                 "p (b c) -> p b c", c=128)[:, 0:3:2, :],
                                             msk.ap().rearrange(
                                                 "p (v b c) -> p v b c", v=3, c=128)
                                             [:, mvar, :, :],
                                             psc.ap().rearrange(
                                                 "p (b c) -> p b c", c=128)[:, 0:3:2, :],
                                             ALU.add),
                                         wait=[ev_sc, ev_ld], inc=True)
                            pscrw.wrote(ev_mk)
                            ev_rm = P.op("ve", lambda v, sl=sl, psc=psc: v.reduce_max(
                                nm4[:, sl:sl + 1], psc[:, 0:KW],
                                axis=AX.X, negate=True),
                                wait=[ev_mk, exp_prev[sl]], inc=True)
                            pscrw.read(ev_rm)
                            ev_ex = P.op("se", lambda s, sl=sl, slb=slb, psc=psc:
                                         s.activation(
                                             pbuf[slb][:], psc[:, 0:KW], AF.Exp,
                                             bias=nm4[:, sl:sl + 1],
                                             accum_out=ds4[:, sl:sl + 1]),
                                         wait=[ev_rm, quad_prev[0]]
                                         + pbrw[slb].wdep(), inc=True)
                            pscrw.read(ev_ex)
                            pbrw[slb].wrote(ev_ex)
                            exp_prev[sl] = ev_ex
                            dg = idx % 8
                            st_a[idx] = (ev_ex, kvg, qt, slb, dg, ph, rbase)
                            if sl == 3:
                                hh0 = hh - 3
                                ev_nm4 = P.op("ve", lambda v, hh0=hh0: v.tensor_add(
                                    es4[:], snko[:, hh0:hh0 + 4], nm4[:]),
                                    wait=[ev_rm, quad_prev[1]], inc=True)
                                ev_es4 = P.op("se", lambda s: s.activation(
                                    es4[:], es4[:], AF.Exp),
                                    wait=[ev_nm4, ev_ex], inc=True)
                                P.op("ve", lambda v: v.tensor_add(
                                    ds4[:], ds4[:], es4[:]), wait=[ev_es4])
                                ev_rc4 = P.op("ve", lambda v: v.reciprocal(
                                    ds4[:], ds4[:]), inc=True)
                                ev_dgl = None
                                for j in range(4):
                                    dgj = (idx - 3 + j) % 8
                                    ev_dgl = P.op("ve", lambda v, j=j, dgj=dgj:
                                                  v.tensor_scalar_mul(
                                                      diag[dgj][:], id_f[:],
                                                      ds4[:, j:j + 1]),
                                                  wait=[ev_rc4] + dgrw[dgj].wdep(),
                                                  inc=True)
                                    dgrw[dgj].wrote(ev_dgl)
                                    st_a[idx - 3 + j] += (ev_dgl,)
                                quad_prev[0] = ev_dgl
                                quad_prev[1] = ev_es4

                        def stage_b(idx):
                            ev_ex, kvg, qt, slb, dg, ph, rbase, ev_dg = st_a.pop(idx)
                            sl = idx % 4
                            ptp, ptprw = pool[idx % 2]
                            ev_tp = None
                            for jj in range(3):
                                ev_tp = P.op("pe", lambda t, slb=slb, jj=jj, ptp=ptp,
                                             dg=dg: t.matmul(
                                                 ptp[:, jj * 128:(jj + 1) * 128],
                                                 pbuf[slb][:, jj * 128:(jj + 1) * 128],
                                                 diag[dg][:], start=True, stop=True),
                                             wait=([ev_ex, ev_dg] + ptprw.wdep())
                                             if jj == 0 else [],
                                             inc=(jj == 2))
                            dgrw[dg].read(ev_tp)
                            ptprw.wrote(ev_tp)
                            pbrw[slb].read(ev_tp)
                            ev_pt = P.op("se", lambda ss, sl=sl, ptp=ptp:
                                         ss.copy(ptb[sl][:], ptp[:, 0:KW]),
                                         wait=[ev_tp] + ptrw[sl].wdep(), inc=True)
                            ptprw.read(ev_pt)
                            ptrw[sl].wrote(ev_pt)
                            st_b[idx] = (ev_pt, kvg, qt, sl, ph, rbase)

                        def stage_c(idx):
                            nonlocal ev_ao
                            ev_pt, kvg, qt, sl, ph, rbase = st_b.pop(idx)
                            qq = idx // H
                            rb = rbase
                            ev_av = None
                            for jj in range(3):
                                ev_av = P.op("pe", lambda t, jj=jj, qt=qt, kvg=kvg,
                                             sl=sl, rb=rb: t.matmul(
                                                 pav[rb:rb + 64, 0:128],
                                                 v_sb[:, (qt + jj) * 512 + kvg * 64:
                                                      (qt + jj) * 512 + kvg * 64 + 64],
                                                 ptb[sl][:, jj * 128:(jj + 1) * 128],
                                                 start=(jj == 0), stop=(jj == 2)),
                                             wait=([ev_pt,
                                                    ev_halo if qt in (0, NTT - 3)
                                                    else ev_kv_local]
                                                   + (pavrw.wdep() if rb == 0
                                                      else []))
                                             if jj == 0 else [],
                                             inc=(jj == 2))
                            ptrw[sl].read(ev_av)
                            if rb == 0:
                                pavrw.wrote(ev_av)
                            else:
                                ev_ao = P.op("se", lambda ss, ph=ph, qq=qq, pav=pav:
                                             ss.copy(
                                                 aout[:, ph * QG + qq * 128:
                                                      ph * QG + qq * 128 + 128],
                                                 pav[:, 0:128]),
                                             wait=[ev_av], inc=True)
                                pavrw.read(ev_ao)

                        SKEW2 = 3
                        for idx in range(2 * H + SKEW + SKEW2):
                            if idx < 2 * H:
                                stage_a(idx)
                            if SKEW <= idx < 2 * H + SKEW:
                                stage_b(idx - SKEW)
                            if idx >= SKEW + SKEW2:
                                stage_c(idx - SKEW - SKEW2)
                        ev_aout = ev_ao
                        if "aout" in taps and g == 0:
                            tap("aout", aout, [128, NMQ * QG], [ev_aout])
                        for mf in range(NMQ):
                            sl = mf % 3
                            ev_w = P.op("gp", lambda gg, mf=mf, sl=sl: gg.dma_start(
                                out=wblk[sl][:], in_=d_wo[mf]),
                                wait=wblkrw[sl].wdep(), inc=True)
                            wblkrw[sl].wrote(ev_w)
                            pp, pprw = pool[2 + mf % 2]
                            ev_m = None
                            for k in range(NMQ):
                                ev_m = P.op("pe", lambda t, k=k, sl=sl, pp=pp:
                                            t.matmul(
                                                pp[:, 0:QG],
                                                wblk[sl][:, k * 128:(k + 1) * 128],
                                                aout[:, k * QG:(k + 1) * QG],
                                                start=(k == 0), stop=(k == NMQ - 1)),
                                            wait=(wblkrw[sl].rdep() + pprw.wdep()
                                                  + [ev_aout]) if k == 0 else [],
                                            inc=(k == NMQ - 1))
                            pprw.wrote(ev_m)
                            wblkrw[sl].read(ev_m)
                            ev_x = P.op("ve", lambda v, mf=mf, g0=g0, pp=pp:
                                        v.scalar_tensor_tensor(
                                            x_sb[:, mf * TS + g0: mf * TS + g0 + QG],
                                            pp[:, 0:QG], bo_sb[:, mf:mf + 1],
                                            x_sb[:, mf * TS + g0: mf * TS + g0 + QG],
                                            op0=ALU.add, op1=ALU.add),
                                        wait=[ev_m], inc=True)
                            pprw.read(ev_x)
                if "x2" in taps:
                    P.barrier()
                    tap("x2", x_sb, [128, NF * TS], [])
                P.barrier()

        if upto == "all":
            with ExitStack() as ctx:
                emit_ffn(ctx, "f2", 0, OWN)
                P.barrier()

        with ExitStack() as ctx:
            pool = alloc_psum_pool(ctx, "out")
            ob = [ctx.enter_context(nc.sbuf_tensor(f"ob{i}", [128, DM], F32))
                  for i in range(2)]
            obrw = [RW("ob0"), RW("ob1")]
            ev_st = None
            for qt in range(8):
                sl = qt % 2
                ev_e = None
                for f in range(NF):
                    pp, pprw = pool[f % 4]
                    ev_t = P.op("pe", lambda t, f=f, qt=qt, pp=pp: t.transpose(
                        pp[:, 0:128],
                        x_sb[:, f * TS + qt * 128:
                             f * TS + (qt + 1) * 128],
                        id_f[:]),
                        wait=pprw.wdep(), inc=True)
                    pprw.wrote(ev_t)
                    ev_e = P.op("ve", lambda v, f=f, sl=sl, pp=pp: v.tensor_copy(
                        ob[sl][:, f * 128:(f + 1) * 128], pp[:, 0:128]),
                        wait=[ev_t] + (obrw[sl].wdep() if f == 0 else []), inc=True)
                    pprw.read(ev_e)
                obrw[sl].wrote(ev_e)
                ev_st = P.op("sy", lambda s, qt=qt, sl=sl: s.dma_start(
                    out=d_y[qt * 128:(qt + 1) * 128, :], in_=ob[sl][:]),
                    wait=[ev_e], inc=True)
                obrw[sl].read(ev_st)
            P.op("ve", None, wait=[ev_st])

        with nc.Block() as block:
            @block.tensor
            def _(t):
                P.emit(t, "pe", sems)

            @block.vector
            def _(v):
                P.emit(v, "ve", sems)

            @block.scalar
            def _(s):
                P.emit(s, "se", sems)

            @block.sync
            def _(s):
                P.emit(s, "sy", sems)

            @block.gpsimd
            def _(g):
                P.emit(g, "gp", sems)

    return nc, d_taps


def _retile_lhsT(W):
    K, M = W.shape
    a = W.reshape(K // 128, 128, M // 128, 128).transpose(2, 1, 0, 3)
    return np.ascontiguousarray(a.reshape(M // 128, 128, K))


def prep_inputs(L0_attn_hidden_states, L0_attn_Wq, L0_attn_bq, L0_attn_Wk,
                L0_attn_bk, L0_attn_Wv, L0_attn_bv, L0_attn_Wo, L0_attn_bo,
                L0_attn_sinks, L0_ffn_gate_w, L0_ffn_gate_b, L0_ffn_expert_bias,
                L0_ffn_w_gate_up, L0_ffn_w_down):
    f32 = np.float32
    x = np.asarray(L0_attn_hidden_states, f32).reshape(S, DM)
    xT = np.ascontiguousarray(x.T)

    wgu = np.asarray(L0_ffn_w_gate_up, f32)
    a = wgu.reshape(NE, 2, NPAIR, 128, NF, 128).transpose(0, 2, 1, 5, 4, 3)
    arr_gu = np.ascontiguousarray(a.reshape(NE, GUBLK, 128, 2048))

    wdn = np.asarray(L0_ffn_w_down, f32)
    arr_wd = np.ascontiguousarray(
        wdn.reshape(NE, NDM, 128, NPAIR, 128).transpose(0, 1, 4, 3, 2)
        .reshape(NE, NDM, 128, 2816))

    hd_perm = np.empty(H * D, dtype=np.int64)
    for h in range(H):
        tile_ = (h // 8) * 4 + (h % 4)
        base_ = ((h // 4) % 2) * 64
        hd_perm[h * D:(h + 1) * D] = tile_ * 128 + base_ + np.arange(D)
    wq_flat = np.asarray(L0_attn_Wq, f32).reshape(DM, H * D)
    wq_p = np.empty_like(wq_flat)
    wq_p[:, hd_perm] = wq_flat
    wo_flat = np.asarray(L0_attn_Wo, f32).reshape(H * D, DM)
    wo_p = np.empty_like(wo_flat)
    wo_p[hd_perm, :] = wo_flat
    wq_a = _retile_lhsT(wq_p)
    wk_a = _retile_lhsT(np.asarray(L0_attn_Wk, f32).reshape(DM, HKV * D))
    wo_a = _retile_lhsT(wo_p)
    wv_n = np.ascontiguousarray(np.asarray(L0_attn_Wv, f32).reshape(DM, HKV * D))

    gw = np.asarray(L0_ffn_gate_w, f32)
    gw_a = np.ascontiguousarray(gw.T.reshape(NF, 128, NE).transpose(1, 0, 2)
                                .reshape(128, NF * NE))

    def tile_col(b):
        return np.ascontiguousarray(np.asarray(b, f32).reshape(-1, 128).T)

    bq_flat = np.asarray(L0_attn_bq, f32).reshape(H * D)
    bq_p = np.empty_like(bq_flat)
    bq_p[hd_perm] = bq_flat
    bq_t = tile_col(bq_p)
    bk_t = tile_col(np.asarray(L0_attn_bk, f32).reshape(HKV * D))
    bo_t = tile_col(np.asarray(L0_attn_bo, f32).reshape(DM))
    bv_bc = np.ascontiguousarray(np.broadcast_to(
        np.asarray(L0_attn_bv, f32).reshape(1, HKV * D), (128, HKV * D)))
    gb_bc = np.ascontiguousarray(np.broadcast_to(
        np.asarray(L0_ffn_gate_b, f32).reshape(1, NE), (128, NE)))
    eb_bc = np.ascontiguousarray(np.broadcast_to(
        np.asarray(L0_ffn_expert_bias, f32).reshape(1, NE), (128, NE)))
    snk = np.ascontiguousarray(np.broadcast_to(
        np.asarray(L0_attn_sinks, f32).reshape(1, H), (128, H)))
    snkn = np.ascontiguousarray(-snk)
    hh_order = []
    for hh in range(H):
        ph, sub = hh // 2, hh % 2
        hh_order.append((ph // 4) * 8 + sub * 4 + (ph % 4))
    snko = np.ascontiguousarray(snk[:, hh_order])

    id_f = np.eye(128, dtype=f32)
    sel = np.zeros((NE, NE * 128), dtype=f32)
    for e in range(NE):
        sel[e, e * 128:(e + 1) * 128] = 1.0
    swp = np.zeros((128, 128), dtype=f32)
    for i in range(0, 128, 2):
        swp[i, i + 1] = 1.0
        swp[i + 1, i] = 1.0

    inv_freq = (1.0 / (WL ** (np.arange(D // 2, dtype=f32) * 2.0 / D))).astype(f32)
    per_core = []
    for c in range(NCORE):
        lo = c * OWN - HALO
        xT_own = np.ascontiguousarray(xT[:, c * OWN:(c + 1) * OWN])

        gpos = (c * OWN + np.arange(OWN)).astype(f32)
        ang = gpos[None, :] * inv_freq[:, None]
        cos32 = np.cos(ang).astype(f32) * RSCALE
        sin32 = np.sin(ang).astype(f32) * RSCALE
        dd = np.arange(128)
        ii = (dd % 64) // 2
        sgn = np.where(dd % 2 == 0, -1.0, 1.0).astype(f32)
        ctab = np.ascontiguousarray(cos32[ii])
        stab = np.ascontiguousarray(sin32[ii] * sgn[:, None])

        qi = np.arange(128)[:, None]
        j = np.arange(KW)[None, :]
        band = (j - qi >= 0) & (j - qi <= 2 * SW)
        v0 = np.where(band, 0.0, NEG).astype(f32)
        if c == 0:
            left_ok = np.broadcast_to((lo + j) >= 0, band.shape)
            v1 = np.where(band & left_ok, 0.0, NEG).astype(f32)
        else:
            v1 = v0
        if c == NCORE - 1:
            right_ok = np.broadcast_to((lo + (NTT - 3) * 128 + j) < S, band.shape)
            v2 = np.where(band & right_ok, 0.0, NEG).astype(f32)
        else:
            v2 = v0
        msk = np.ascontiguousarray(np.concatenate(
            [np.concatenate([v[:, 0:128], v[:, 256:KW]], axis=1)
             for v in (v0, v1, v2)], axis=1))
        pp_ = np.arange(128, dtype=np.int32)
        left_slab = ((c - 1) % NCORE) * 2 + 1
        right_slab = ((c + 1) % NCORE) * 2 + 0
        hidx = np.stack([left_slab * 128 + pp_, right_slab * 128 + pp_],
                        axis=1).astype(np.int32)
        per_core.append({"xT": xT_own, "ctab": ctab, "stab": stab, "msk": msk,
                         "hidx": hidx})

    shared = {
        "wgu": arr_gu, "wdn": arr_wd, "wq": wq_a, "wk": wk_a, "wv": wv_n,
        "wo": wo_a, "gw": gw_a, "bq": bq_t, "bk": bk_t, "bv": bv_bc, "bo": bo_t,
        "gb": gb_bc, "eb": eb_bc, "snk": snk, "snkn": snkn, "snko": snko, "idf": id_f,
        "swp": swp, "sel": sel,
    }
    in_maps = []
    for c in range(NCORE):
        m = dict(shared)
        m.update(per_core[c])
        in_maps.append(m)
    return in_maps


def kernel(**inputs):
    in_maps = prep_inputs(**inputs)
    nc, _ = build_core_program(upto="all", taps=())
    res = run_bass_kernel_spmd(nc, in_maps, core_ids=list(range(NCORE)))
    y = np.concatenate([r["y"] for r in res.results], axis=0)
    return y.reshape(1, S, DM)


# version 18
# speedup vs baseline: 1.0123x; 1.0123x over previous
import numpy as np
import ml_dtypes
from contextlib import ExitStack
import concourse.bass as bass
import concourse.mybir as mybir
from concourse.bass_utils import run_bass_kernel_spmd
from concourse.bass_core_rust import IndirectOffsetOnAxis

F32 = mybir.dt.float32
BF16 = mybir.dt.bfloat16
AF = mybir.ActivationFunctionType
ALU = mybir.AluOpType
AX = mybir.AxisListType

NCORE = 8
S = 8192
DM = 2048
OWN = 1024
HALO = 128
EXT = OWN + 2 * HALO
NF = DM // 128
NE = 8
INTER = 2816
NPAIR = INTER // 128
GUBLK = 2 * NPAIR
NDM = DM // 128
H, HKV, D = 32, 8, 64
NMQ, NMK = 16, 4
SW = 128
WL = 150000.0
RSCALE = float(D) ** -0.25
NEG = -1.0e9
EPS = 1e-5
NTT = EXT // 128
TS = OWN
QG = 256
NQG = OWN // QG
KW = 384

DMA_Q = ("sy", "gp")
ENGS = ("pe", "ve", "se", "sy", "gp")


def chunk_list(t0, tn, width=512):
    out = []
    o = t0
    while o < t0 + tn:
        w = min(width, t0 + tn - o)
        out.append((o, w))
        o += w
    return out


class Prog:

    def __init__(self):
        self.streams = {e: [] for e in ENGS}
        self.count = {e: 0 for e in ENGS}
        self.mark = {}

    def op(self, eng, fn, wait=(), inc=False):
        waits = []
        for evt in wait:
            if evt is None:
                continue
            p, v = evt
            if p == eng and eng == "pe":
                continue
            k = (eng, p)
            if self.mark.get(k, 0) < v:
                self.mark[k] = v
                waits.append((p, v))
        ev = None
        if inc:
            self.count[eng] += 16 if eng in DMA_Q else 1
            ev = (eng, self.count[eng])
        self.streams[eng].append((tuple(waits), fn, inc))
        return ev

    def barrier(self):
        evs = [(e, self.count[e]) for e in ENGS if self.count[e] > 0]
        for e in ENGS:
            for p, v in evs:
                if p == e:
                    continue
                k = (e, p)
                if self.mark.get(k, 0) < v:
                    self.mark[k] = v
                    self.streams[e].append((((p, v),), None, False))

    def emit(self, eng_obj, eng_name, sems):
        step = 16 if eng_name in DMA_Q else 1
        for waits, fn, inc in self.streams[eng_name]:
            for p, v in waits:
                eng_obj.wait_ge(sems[p], v)
            if fn is None:
                continue
            ins = fn(eng_obj)
            if inc:
                ins.then_inc(sems[eng_name], step)


class RW:

    def __init__(self, name=""):
        self.name = name
        self.wr = None
        self.rds = []

    def rdep(self):
        return [self.wr] if self.wr is not None else []

    def wdep(self):
        d = list(self.rds)
        if self.wr is not None:
            d.append(self.wr)
        return d

    def wrote(self, ev):
        assert ev is not None, f"RW {self.name}: writer must inc"
        self.wr = ev
        self.rds = []

    def read(self, ev):
        assert ev is not None, f"RW {self.name}: reader must inc"
        self.rds.append(ev)


def build_core_program(upto="all", taps=()):
    nc = bass.Bass("TRN2", target_bir_lowering=False)

    d_x = nc.dram_tensor("xT", [DM, TS], F32, kind="ExternalInput").ap()
    d_gu = nc.dram_tensor("wgu", [NE, GUBLK, 128, 2048], F32, kind="ExternalInput").ap()
    d_wd = nc.dram_tensor("wdn", [NE, NDM, 128, 2816], F32, kind="ExternalInput").ap()
    d_wq = nc.dram_tensor("wq", [NMQ, 128, 2048], F32, kind="ExternalInput").ap()
    d_wk = nc.dram_tensor("wk", [NMK, 128, 2048], F32, kind="ExternalInput").ap()
    d_wv = nc.dram_tensor("wv", [DM, 512], F32, kind="ExternalInput").ap()
    d_wo = nc.dram_tensor("wo", [NMQ, 128, 2048], F32, kind="ExternalInput").ap()
    d_gw = nc.dram_tensor("gw", [128, NF * NE], F32, kind="ExternalInput").ap()
    d_bq = nc.dram_tensor("bq", [128, NMQ], F32, kind="ExternalInput").ap()
    d_bk = nc.dram_tensor("bk", [128, NMK], F32, kind="ExternalInput").ap()
    d_bv = nc.dram_tensor("bv", [128, 512], F32, kind="ExternalInput").ap()
    d_bo = nc.dram_tensor("bo", [128, NMQ], F32, kind="ExternalInput").ap()
    d_gb = nc.dram_tensor("gb", [128, NE], F32, kind="ExternalInput").ap()
    d_eb = nc.dram_tensor("eb", [128, NE], F32, kind="ExternalInput").ap()
    d_snk = nc.dram_tensor("snk", [128, H], F32, kind="ExternalInput").ap()
    d_snko = nc.dram_tensor("snko", [128, H], F32, kind="ExternalInput").ap()
    d_snkn = nc.dram_tensor("snkn", [128, H], F32, kind="ExternalInput").ap()
    d_cos = nc.dram_tensor("ctab", [128, TS], F32, kind="ExternalInput").ap()
    d_sin = nc.dram_tensor("stab", [128, TS], F32, kind="ExternalInput").ap()
    d_msk = nc.dram_tensor("msk", [128, 3 * 256], F32, kind="ExternalInput").ap()
    d_idf = nc.dram_tensor("idf", [128, 128], F32, kind="ExternalInput").ap()
    d_swp = nc.dram_tensor("swp", [128, 128], F32, kind="ExternalInput").ap()
    d_sel = nc.dram_tensor("sel", [NE, NE * 128], F32, kind="ExternalInput").ap()
    d_hidx = nc.dram_tensor("hidx", [128, 2], mybir.dt.int32,
                            kind="ExternalInput").ap()
    d_ccin = nc.dram_tensor("cc_in", [2, 128, 1024], BF16, kind="Internal").ap()
    d_ccout = nc.dram_tensor("cc_out", [16, 128, 1024], BF16, kind="Internal").ap()
    d_y = nc.dram_tensor("y", [OWN, DM], F32, kind="ExternalOutput").ap()
    d_taps = {}

    P = Prog()

    with ExitStack() as outer:
        E = outer.enter_context
        sems = {e: E(nc.semaphore(f"sem_{e}")) for e in ENGS}

        x_sb = E(nc.sbuf_tensor("x_sb", [128, NF * TS], F32))
        ones_col = E(nc.sbuf_tensor("ones_col", [128, 1], BF16))
        ones_row_f = E(nc.sbuf_tensor("ones_row_f", [1, 128], F32))
        ones_row_b = E(nc.sbuf_tensor("ones_row_b", [1, 128], BF16))
        eps_t = E(nc.sbuf_tensor("eps_t", [1, 1], F32))
        inv_t = E(nc.sbuf_tensor("inv_t", [1, 1], F32))
        half_t = E(nc.sbuf_tensor("half_t", [128, 1], F32))
        id_f = E(nc.sbuf_tensor("id_f", [128, 128], F32))
        gb_bc = E(nc.sbuf_tensor("gb_bc", [128, NE], F32))
        eb_bc = E(nc.sbuf_tensor("eb_bc", [128, NE], F32))
        gw_bf = E(nc.sbuf_tensor("gw_bf", [128, NF * NE], BF16))
        sel_bf = E(nc.sbuf_tensor("sel_bf", [NE, NE * 128], BF16))

        ev_x0 = P.op("sy", lambda s: s.dma_start(
            out=x_sb.ap().rearrange("p (f t) -> p f t", f=NF),
            in_=d_x.rearrange("(f p) t -> p f t", p=128)), inc=True)
        ev_misc = None
        for dst, src in ((id_f, d_idf), (gb_bc, d_gb), (eb_bc, d_eb)):
            ev_misc = P.op("sy", lambda s, a=dst, b=src:
                           s.dma_start(out=a[:], in_=b), inc=True)
        P.op("gp", lambda g: g.dma_start(out=gw_bf[:], in_=d_gw), inc=True)
        P.op("gp", lambda g: g.dma_start(out=sel_bf[:], in_=d_sel), inc=True)
        for t, vv in ((ones_col, 1.0), (ones_row_f, 1.0), (ones_row_b, 1.0),
                      (eps_t, EPS), (inv_t, 1.0 / DM), (half_t, 0.5)):
            P.op("ve", lambda v, a=t, c=vv: v.memset(a[:], c), inc=True)
        P.barrier()

        def alloc_psum_pool(ctx, tag):
            return [(ctx.enter_context(nc.psum_tensor(f"ps_{tag}_{i}", [128, 512], F32)),
                     RW(f"ps_{tag}_{i}")) for i in range(8)]

        def emit_ln(ctx, tag, lnT, tok0, tokn, pool):
            ev_out = None
            ev_chunks = {}
            with ExitStack() as c2:
                C = c2.enter_context
                xb = [C(nc.sbuf_tensor(f"xb{tag}{i}", [128, 512], BF16)) for i in range(2)]
                sq = [C(nc.sbuf_tensor(f"sq{tag}{i}", [128, 512], BF16)) for i in range(2)]
                row_m = C(nc.sbuf_tensor(f"rm{tag}", [1, 512], F32))
                row_b = C(nc.sbuf_tensor(f"rb{tag}", [1, 512], F32))
                row_c = C(nc.sbuf_tensor(f"rc{tag}", [1, 512], F32))
                ln_tmp = C(nc.sbuf_tensor(f"lt{tag}", [128, 512], F32))
                xbrw = [RW("xb0"), RW("xb1")]
                pst, strw = pool[6]
                pss, ssrw = pool[5]
                pra, prarw = pool[0]
                prb, prbrw = pool[1]
                for (c0, cn) in chunk_list(tok0, tokn):
                    ev_mm = None
                    for k in range(NF):
                        sl = k % 2
                        xcol = x_sb[:, k * TS + c0: k * TS + c0 + cn]
                        evc = P.op("se", lambda s, a=xb[sl], b=xcol, n=cn:
                                   s.copy(a[:, 0:n], b),
                                   wait=xbrw[sl].wdep(), inc=True)
                        evs = P.op("se", lambda s, a=sq[sl], b=xcol, n=cn:
                                   s.activation(a[:, 0:n], b, AF.Square), inc=True)
                        P.op("pe", lambda t, a=xb[sl], n=cn, k=k:
                             t.matmul(pst[0:1, 0:n], ones_col[:], a[:, 0:n],
                                      start=(k == 0), stop=(k == NF - 1)),
                             wait=[evs] + ((strw.wdep() + ssrw.wdep())
                                           if k == 0 else []))
                        ev_mm = P.op("pe", lambda t, a=sq[sl], n=cn, k=k:
                                     t.matmul(pss[0:1, 0:n], ones_col[:], a[:, 0:n],
                                              start=(k == 0), stop=(k == NF - 1)),
                                     inc=True)
                        xbrw[sl].wrote(ev_mm)
                    strw.wrote(ev_mm)
                    ssrw.wrote(ev_mm)
                    ev_q1 = P.op("ve", lambda v, n=cn: v.tensor_scalar_mul(
                        row_m[0:1, 0:n], pst[0:1, 0:n], inv_t[0:1, 0:1]),
                        wait=[ev_mm], inc=True)
                    ev = P.op("ve", lambda v, n=cn: v.tensor_scalar_mul(
                        row_b[0:1, 0:n], pss[0:1, 0:n], inv_t[0:1, 0:1]), inc=True)
                    strw.read(ev)
                    ssrw.read(ev)
                    P.op("ve", lambda v, n=cn: v.tensor_mul(
                        row_c[0:1, 0:n], row_m[0:1, 0:n], row_m[0:1, 0:n]),
                        wait=[ev_q1, ev])
                    ev_var = P.op("ve", lambda v, n=cn: v.tensor_sub(
                        row_b[0:1, 0:n], row_b[0:1, 0:n], row_c[0:1, 0:n]), inc=True)
                    ev_sd = P.op("se", lambda s, n=cn: s.activation(
                        row_b[0:1, 0:n], row_b[0:1, 0:n], AF.Sqrt,
                        bias=eps_t[0:1, 0:1]), wait=[ev_var], inc=True)
                    ev_rcp = P.op("ve", lambda v, n=cn: v.reciprocal(
                        row_b[0:1, 0:n], row_b[0:1, 0:n]), wait=[ev_sd], inc=True)
                    ev_rows = P.op("ve", lambda v, n=cn: v.tensor_mul(
                        row_c[0:1, 0:n], row_m[0:1, 0:n], row_b[0:1, 0:n]),
                        wait=[ev_rcp], inc=True)
                    ev_b1 = P.op("pe", lambda t, n=cn: t.matmul(
                        pra[:, 0:n], ones_row_f[:], row_b[0:1, 0:n],
                        start=True, stop=True),
                        wait=[ev_rows] + prarw.wdep(), inc=True)
                    prarw.wrote(ev_b1)
                    ev_b2 = P.op("pe", lambda t, n=cn: t.matmul(
                        prb[:, 0:n], ones_row_f[:], row_c[0:1, 0:n],
                        start=True, stop=True), wait=prbrw.wdep(), inc=True)
                    prbrw.wrote(ev_b2)
                    ev_ln = None
                    for k in range(NF):
                        P.op("ve", lambda v, k=k, c0=c0, n=cn: v.tensor_mul(
                            ln_tmp[:, 0:n],
                            x_sb[:, k * TS + c0: k * TS + c0 + n], pra[:, 0:n]),
                            wait=[ev_b1, ev_b2] if k == 0 else [])
                        ev_ln = P.op("ve", lambda v, k=k, c0=c0, n=cn: v.tensor_sub(
                            lnT[:, k * TS + c0: k * TS + c0 + n],
                            ln_tmp[:, 0:n], prb[:, 0:n]),
                            inc=(k == NF - 1))
                    prarw.read(ev_ln)
                    prbrw.read(ev_ln)
                    ev_chunks[c0] = ev_ln
                    ev_out = ev_ln
            return ev_out, ev_chunks

        def emit_router(ctx, tag, lnT, comb_em, tok0, tokn, pool, ev_lnT_or):
            ph, phrw = pool[7]
            ev_ce = None
            with ExitStack() as c2:
                C = c2.enter_context
                r_tm = C(nc.sbuf_tensor(f"rtm{tag}", [128, NE], F32))
                r_t2 = C(nc.sbuf_tensor(f"rt2{tag}", [128, NE], F32))
                m8 = C(nc.sbuf_tensor(f"m8{tag}", [128, 8], F32))
                nmr = C(nc.sbuf_tensor(f"nmr{tag}", [128, 1], F32))
                dsr = C(nc.sbuf_tensor(f"dsr{tag}", [128, 1], F32))
                for ti in range(tok0 // 128, (tok0 + tokn) // 128):
                    ev_mm = None
                    for k in range(NF):
                        ev_mm = P.op("pe", lambda t, k=k, ti=ti: t.matmul(
                            ph[:, 0:NE],
                            lnT[:, k * TS + ti * 128: k * TS + (ti + 1) * 128],
                            gw_bf[:, k * NE:(k + 1) * NE],
                            start=(k == 0), stop=(k == NF - 1)),
                            wait=([ev_lnT_or(ti)] + phrw.wdep()) if k == 0 else [],
                            inc=(k == NF - 1))
                    phrw.wrote(ev_mm)
                    ev1 = P.op("ve", lambda v: v.tensor_add(
                        r_tm[:], ph[:, 0:NE], gb_bc[:]), wait=[ev_mm], inc=True)
                    phrw.read(ev1)
                    if f"rlg{tag}" in taps and ti == tok0 // 128:
                        tap(f"rlg{tag}", r_tm, [128, NE], [ev1])
                    ev_nm = P.op("ve", lambda v: v.reduce_max(
                        nmr[:], r_tm[:], axis=AX.X, negate=True),
                        wait=[ev1], inc=True)
                    ev_ex = P.op("se", lambda s: s.activation(
                        r_tm[:], r_tm[:], AF.Exp, bias=nmr[:, 0:1],
                        accum_out=dsr[:]), wait=[ev_nm], inc=True)
                    ev_rc = P.op("ve", lambda v: v.reciprocal(dsr[:], dsr[:]),
                                 wait=[ev_ex], inc=True)
                    if f"rds{tag}" in taps and ti == tok0 // 128:
                        tap(f"rds{tag}", dsr, [128, 1], [ev_rc])
                        tap(f"rex{tag}", r_tm, [128, NE], [ev_rc])
                    ev_sc = P.op("ve", lambda v: v.tensor_scalar_mul(
                        r_tm[:], r_tm[:], dsr[:, 0:1]),
                        wait=[ev_rc], inc=True)
                    if f"rds{tag}" in taps and ti == tok0 // 128:
                        tap(f"rsc{tag}", r_tm, [128, NE], [ev_sc])
                    ev_eb = P.op("ve", lambda v: v.tensor_add(
                        r_tm[:], r_tm[:], eb_bc[:]), wait=[ev_sc], inc=True)
                    if f"rsm{tag}" in taps and ti == tok0 // 128:
                        tap(f"rsm{tag}", r_tm, [128, NE], [ev_eb])
                    ev_m8 = P.op("ve", lambda v: v.max(out=m8[:], in_=r_tm[:]),
                                 wait=[ev_eb], inc=True)
                    ev_st2 = P.op("ve", lambda v: v.scalar_tensor_tensor(
                        r_t2[:], r_tm[:], m8[:, 1:2], r_tm[:],
                        op0=ALU.is_ge, op1=ALU.mult), wait=[ev_m8], inc=True)
                    ev_cb = P.op("ve", lambda v: v.tensor_scalar_mul(
                        r_t2[:], r_t2[:], half_t[:, 0:1]), wait=[ev_st2], inc=True)
                    ev_tp = P.op("pe", lambda t: t.matmul(
                        ph[0:NE, 0:128], r_t2[:], id_f[:], start=True, stop=True),
                        wait=[ev_cb, ev_misc] + phrw.wdep(), inc=True)
                    phrw.wrote(ev_tp)
                    ev_ce = P.op("ve", lambda v, ti=ti: v.tensor_copy(
                        comb_em[0:NE, ti * 128:(ti + 1) * 128], ph[0:NE, 0:128]),
                        wait=[ev_tp], inc=True)
                    phrw.read(ev_ce)
            return ev_ce

        def tap(name, sbuf, shape, wait):
            dd = nc.dram_tensor("tap_" + name, shape, sbuf.dtype,
                                kind="ExternalOutput").ap()
            d_taps[name] = dd
            P.op("sy", lambda s: s.dma_start(out=dd,
                                             in_=sbuf[0:shape[0], 0:shape[1]]),
                 wait=wait, inc=True)

        def emit_ffn(ctx, tag, tok0, tokn, stage="full"):
            pool = alloc_psum_pool(ctx, tag)
            lnT = ctx.enter_context(nc.sbuf_tensor(f"lnT{tag}", [128, NF * TS], BF16))
            comb_em = ctx.enter_context(nc.sbuf_tensor(f"cem{tag}", [NE, TS], BF16))
            wgu = [ctx.enter_context(nc.sbuf_tensor(f"wgu{tag}{i}", [128, 2048], BF16))
                   for i in range(4)]
            wdn = [ctx.enter_context(nc.sbuf_tensor(f"wdn{tag}{i}", [128, 1408], BF16))
                   for i in range(2)]
            inter = ctx.enter_context(
                nc.sbuf_tensor(f"inter{tag}", [128, NPAIR * TS], BF16))
            sil = [ctx.enter_context(nc.sbuf_tensor(f"sil{tag}0", [128, 512], F32))]
            cbc = ctx.enter_context(nc.sbuf_tensor(f"cbc{tag}", [128, TS], BF16))

            ev_lnT, ev_lnc = emit_ln(ctx, tag, lnT, tok0, tokn, pool)

            def lnc(c0):
                ks = [k for k in ev_lnc if k <= c0]
                return ev_lnc[max(ks)] if ks else ev_lnT
            if f"lnT{tag}" in taps:
                tap(f"lnT{tag}", lnT, [128, NF * TS], [ev_lnT])
            if stage == "ln":
                return ev_lnT
            ev_comb = emit_router(ctx, tag, lnT, comb_em, tok0, tokn, pool,
                                  lambda ti: lnc(ti * 128))
            if f"cem{tag}" in taps:
                tap(f"cem{tag}", comb_em, [NE, TS], [ev_comb])
            if stage == "router":
                return ev_comb

            chunks = chunk_list(tok0, tokn)
            nch = len(chunks)
            wgurw = [RW(f"wgu{i}") for i in range(4)]
            wdnrw = [RW(f"wdn{i}") for i in range(2)]
            silrw = [RW("sil0")]
            cbcrw = RW("cbc")
            pc, pcrw = pool[7]

            def gu_dma(e, blk):
                sl = blk % 4
                ev = P.op("gp", lambda g, e=e, blk=blk, sl=sl: g.dma_start(
                    out=wgu[sl][:], in_=d_gu[e, blk]),
                    wait=wgurw[sl].wdep(), inc=True)
                wgurw[sl].wrote(ev)

            def wd_dma(e, hb):
                m, half = hb // 2, hb % 2
                sl = hb % 2
                ev = P.op("gp", lambda g, e=e, m=m, half=half, sl=sl: g.dma_start(
                    out=wdn[sl][:], in_=d_wd[e, m, :, half * 1408:(half + 1) * 1408]),
                    wait=wdnrw[sl].wdep(), inc=True)
                wdnrw[sl].wrote(ev)

            ev_x_last = None
            dn_ctr = 0
            for e in range(NE):
                for (c0, cn) in chunks:
                    ev = P.op("pe", lambda t, e=e, c0=c0, cn=cn: t.matmul(
                        pc[:, 0:cn], sel_bf[:, e * 128:(e + 1) * 128],
                        comb_em[0:NE, c0:c0 + cn],
                        start=True, stop=True),
                        wait=[ev_comb] + pcrw.wdep(), inc=True)
                    pcrw.wrote(ev)
                    ev = P.op("ve", lambda v, c0=c0, cn=cn: v.tensor_copy(
                        cbc[:, c0:c0 + cn], pc[:, 0:cn]),
                        wait=[ev] + (cbcrw.wdep() if c0 == chunks[0][0] else []),
                        inc=True)
                    pcrw.read(ev)
                ev_cbc = ev
                cbcrw.wrote(ev_cbc)
                gu_dma(e, 0)
                gu_dma(e, 1)
                gu_dma(e, 2)
                ev_int_c = {}
                gu_ctr = 0
                for i in range(NPAIR):
                    gsl, usl = (2 * i) % 4, (2 * i + 1) % 4
                    pus, pgs = [], []
                    for (c0, cn) in chunks:
                        pu, purw = pool[gu_ctr % 6]
                        gu_ctr += 1
                        ev_u = None
                        for k in range(NF):
                            ev_u = P.op("pe", lambda t, k=k, usl=usl, c0=c0, cn=cn,
                                        pu=pu: t.matmul(
                                            pu[:, 0:cn],
                                            wgu[usl][:, k * 128:(k + 1) * 128],
                                            lnT[:, k * TS + c0: k * TS + c0 + cn],
                                            start=(k == 0), stop=(k == NF - 1)),
                                        wait=(wgurw[usl].rdep() + purw.wdep()
                                              + [lnc(c0)]) if k == 0 else [],
                                        inc=(k == NF - 1))
                        purw.wrote(ev_u)
                        pus.append((pu, purw, ev_u))
                    wgurw[usl].read(pus[-1][2])
                    for ci, (c0, cn) in enumerate(chunks):
                        pg, pgrw = pool[gu_ctr % 6]
                        gu_ctr += 1
                        ev_g = None
                        for k in range(NF):
                            ev_g = P.op("pe", lambda t, k=k, gsl=gsl, c0=c0, cn=cn,
                                        pg=pg: t.matmul(
                                            pg[:, 0:cn],
                                            wgu[gsl][:, k * 128:(k + 1) * 128],
                                            lnT[:, k * TS + c0: k * TS + c0 + cn],
                                            start=(k == 0), stop=(k == NF - 1)),
                                        wait=(wgurw[gsl].rdep() + pgrw.wdep())
                                        if k == 0 else [],
                                        inc=(k == NF - 1))
                        pgrw.wrote(ev_g)
                        pgs.append((pg, pgrw, ev_g))
                        ssl = 0
                        pu, purw, ev_u = pus[ci]
                        ev_s = P.op("se", lambda s, ssl=ssl, cn=cn, pg=pg:
                                    s.activation(sil[ssl][:, 0:cn], pg[:, 0:cn],
                                                 AF.Silu),
                                    wait=[ev_g] + silrw[ssl].wdep(), inc=True)
                        pgrw.read(ev_s)
                        silrw[ssl].wrote(ev_s)
                        ev_t = P.op("ve", lambda v, ssl=ssl, i=i, c0=c0, cn=cn,
                                    pu=pu: v.tensor_mul(
                                        inter[:, i * TS + c0: i * TS + c0 + cn],
                                        sil[ssl][:, 0:cn], pu[:, 0:cn]),
                                    wait=[ev_s, ev_u], inc=True)
                        purw.read(ev_t)
                        silrw[ssl].read(ev_t)
                        ev_i = P.op("ve", lambda v, i=i, c0=c0, cn=cn: v.tensor_mul(
                            inter[:, i * TS + c0: i * TS + c0 + cn],
                            inter[:, i * TS + c0: i * TS + c0 + cn],
                            cbc[:, c0:c0 + cn]),
                            wait=[ev_cbc], inc=True)
                        cbcrw.read(ev_i)
                        ev_int_c[ci] = ev_i
                    wgurw[gsl].read(pgs[-1][2])
                    if 2 * i + 3 < GUBLK:
                        gu_dma(e, 2 * i + 3)
                    if 2 * i + 4 < GUBLK:
                        gu_dma(e, 2 * i + 4)
                wd_dma(e, 0)
                wd_dma(e, 1)
                for m in range(NDM):
                    bset = [4, 5, 6] if m % 2 == 0 else [0, 1, 2]
                    ev_d = {}
                    for k in range(NPAIR):
                        msl = (2 * m + k // 11) % 2
                        kk = k % 11
                        for ci, (c0, cn) in enumerate(chunks):
                            ev_d[ci] = P.op("pe", lambda t, k=k, kk=kk, msl=msl,
                                            c0=c0, cn=cn, ci=ci, bset=bset: t.matmul(
                                                pool[bset[ci]][0][:, 0:cn],
                                                wdn[msl][:, kk * 128:(kk + 1) * 128],
                                                inter[:, k * TS + c0: k * TS + c0 + cn],
                                                start=(k == 0), stop=(k == NPAIR - 1)),
                                            wait=(wdnrw[msl].rdep()
                                                  + (pool[bset[ci]][1].wdep()
                                                     + [ev_int_c[ci]] if k == 0
                                                     else [])) if (k == 0 or kk == 0)
                                            else [],
                                            inc=(k == NPAIR - 1 or kk == 10))
                        if kk == 10:
                            wdnrw[msl].read(ev_d[len(chunks) - 1])
                            if 2 * m + k // 11 + 2 < 2 * NDM:
                                wd_dma(e, 2 * m + k // 11 + 2)
                    for ci, (c0, cn) in enumerate(chunks):
                        pd, pdrw = pool[bset[ci]]
                        pdrw.wrote(ev_d[ci])
                        ev_x = P.op("ve", lambda v, m=m, c0=c0, cn=cn, pd=pd:
                                    v.tensor_add(
                                        x_sb[:, m * TS + c0: m * TS + c0 + cn],
                                        x_sb[:, m * TS + c0: m * TS + c0 + cn],
                                        pd[:, 0:cn]),
                                    wait=[ev_d[ci]], inc=True)
                        pdrw.read(ev_x)
                        ev_x_last = ev_x
            return ev_x_last

        with ExitStack() as ctx:
            emit_ffn(ctx, "f1", 0, OWN,
                     stage={"ln1": "ln", "router1": "router"}.get(upto, "full"))
            if "x1" in taps:
                P.barrier()
                tap("x1", x_sb, [128, NF * TS], [])
            P.barrier()

        if upto in ("attn", "all"):
            with ExitStack() as ctx:
                pool = alloc_psum_pool(ctx, "at")
                CE = ctx.enter_context
                msk = CE(nc.sbuf_tensor("msk_sb", [128, 3 * 256], F32))
                ctab = CE(nc.sbuf_tensor("ctab_sb", [128, TS], F32))
                stab = CE(nc.sbuf_tensor("stab_sb", [128, TS], F32))
                snk = CE(nc.sbuf_tensor("snk_sb", [128, H], F32))
                snko = CE(nc.sbuf_tensor("snko_sb", [128, H], F32))
                snkn = CE(nc.sbuf_tensor("snkn_sb", [128, H], F32))
                bq_sb = CE(nc.sbuf_tensor("bq_sb", [128, NMQ], F32))
                bk_sb = CE(nc.sbuf_tensor("bk_sb", [128, NMK], F32))
                bv_sb = CE(nc.sbuf_tensor("bv_sb", [128, 512], F32))
                bo_sb = CE(nc.sbuf_tensor("bo_sb", [128, NMQ], F32))
                swp_f = CE(nc.sbuf_tensor("swp_f", [128, 128], F32))
                lnT = CE(nc.sbuf_tensor("lnT2", [128, NF * TS], BF16))
                v_sb = CE(nc.sbuf_tensor("v_sb", [128, NTT * 512], BF16))
                krot = CE(nc.sbuf_tensor("krot", [128, NMK * EXT], BF16))
                qrot = CE(nc.sbuf_tensor("qrot", [128, NMQ * OWN], BF16))
                wblk = [CE(nc.sbuf_tensor(f"wblk{i}", [128, 2048], BF16))
                        for i in range(3)]
                wvb = [CE(nc.sbuf_tensor(f"wvb{i}", [128, 512], BF16))
                       for i in range(2)]
                wblkrw = [RW(f"wblk{i}") for i in range(3)]
                wvbrw = [RW("wvb0"), RW("wvb1")]
                hidx_sb = CE(nc.sbuf_tensor("hidx_sb", [128, 2], mybir.dt.int32))
                halo_sb = CE(nc.sbuf_tensor("halo_sb", [128, 2048], BF16))
                cc_sem = CE(nc.semaphore("sem_cc"))
                sems["cc"] = cc_sem
                ev_ld = None
                for dst, src in ((msk, d_msk), (ctab, d_cos), (stab, d_sin),
                                 (snk, d_snk), (snkn, d_snkn), (snko, d_snko),
                                 (bq_sb, d_bq),
                                 (bk_sb, d_bk), (bv_sb, d_bv), (bo_sb, d_bo),
                                 (swp_f, d_swp), (hidx_sb, d_hidx)):
                    ev_ld = P.op("sy", lambda s, a=dst, b=src: s.dma_start(
                        out=a[:], in_=b), inc=True)
                ev_lnT, ev_lnc2 = emit_ln(ctx, "l2", lnT, 0, OWN, pool)
                if "lnT2" in taps:
                    tap("lnT2", lnT, [128, NF * TS], [ev_lnT])

                with ExitStack() as c2:
                    C2 = c2.enter_context
                    kf = [C2(nc.sbuf_tensor(f"kf{i}", [128, 512], F32))
                          for i in range(2)]
                    kt1 = C2(nc.sbuf_tensor("kt1", [128, 512], F32))
                    kt2 = C2(nc.sbuf_tensor("kt2", [128, 512], F32))
                    kfrw = [RW("kf0"), RW("kf1")]
                    pctr = 0
                    ev_krot = None
                    for mb in range(NMK):
                        sl = mb % 3
                        ev_w = P.op("gp", lambda g, mb=mb, sl=sl: g.dma_start(
                            out=wblk[sl][:], in_=d_wk[mb]),
                            wait=wblkrw[sl].wdep(), inc=True)
                        wblkrw[sl].wrote(ev_w)
                        ev_mk_last = None
                        for (c0, cn) in chunk_list(0, OWN):
                            pp, pprw = pool[2 + pctr % 2]
                            ps, psrw = pool[4 + pctr % 2]
                            fsl = pctr % 2
                            pctr += 1
                            ev_m = None
                            for k in range(NF):
                                ev_m = P.op("pe", lambda t, k=k, sl=sl, c0=c0, cn=cn,
                                            pp=pp: t.matmul(
                                                pp[:, 0:cn],
                                                wblk[sl][:, k * 128:(k + 1) * 128],
                                                lnT[:, k * TS + c0:k * TS + c0 + cn],
                                                start=(k == 0), stop=(k == NF - 1)),
                                            wait=(wblkrw[sl].rdep() + pprw.wdep()
                                                  + [ev_lnc2.get(c0, ev_lnT)])
                                            if k == 0 else [],
                                            inc=(k == NF - 1))
                            pprw.wrote(ev_m)
                            ev_mk_last = ev_m
                            ev_f = P.op("se", lambda s, fsl=fsl, mb=mb, cn=cn, pp=pp:
                                        s.activation(kf[fsl][:, 0:cn], pp[:, 0:cn],
                                                     AF.Identity, bias=bk_sb[:, mb:mb + 1]),
                                        wait=[ev_m, ev_ld] + kfrw[fsl].wdep(), inc=True)
                            pprw.read(ev_f)
                            kfrw[fsl].wrote(ev_f)
                            ev_sw = P.op("pe", lambda t, fsl=fsl, cn=cn, ps=ps:
                                         t.matmul(ps[:, 0:cn], swp_f[:],
                                                  kf[fsl][:, 0:cn],
                                                  start=True, stop=True),
                                         wait=[ev_f] + psrw.wdep(), inc=True)
                            psrw.wrote(ev_sw)
                            P.op("ve", lambda v, c0=c0, cn=cn, ps=ps: v.tensor_mul(
                                kt1[:, 0:cn], stab[:, c0:c0 + cn],
                                ps[:, 0:cn]),
                                wait=[ev_sw, ev_ld])
                            ev_r1 = P.op("ve", lambda v, fsl=fsl, c0=c0, cn=cn:
                                         v.tensor_mul(kt2[:, 0:cn], kf[fsl][:, 0:cn],
                                                      ctab[:, c0:c0 + cn]),
                                         inc=True)
                            psrw.read(ev_r1)
                            kfrw[fsl].read(ev_r1)
                            ev_krot = P.op("ve", lambda v, mb=mb, c0=c0, cn=cn:
                                           v.tensor_add(
                                               krot[:, mb * EXT + HALO + c0:
                                                    mb * EXT + HALO + c0 + cn],
                                               kt1[:, 0:cn], kt2[:, 0:cn]), inc=True)
                        wblkrw[sl].read(ev_mk_last)

                    ev_vdone = None
                    for tp in range(OWN // 256):
                        t0, t1 = 2 * tp, 2 * tp + 1
                        pp0, pp0rw = pool[6]
                        pp1, pp1rw = pool[7]
                        ev_m0 = ev_m1 = None
                        for k in range(NF):
                            sl = k % 2
                            ev_w = P.op("gp", lambda g, k=k, sl=sl: g.dma_start(
                                out=wvb[sl][:], in_=d_wv[k * 128:(k + 1) * 128, :]),
                                wait=wvbrw[sl].wdep(), inc=True)
                            wvbrw[sl].wrote(ev_w)
                            ev_m0 = P.op("pe", lambda t, k=k, sl=sl, t0=t0: t.matmul(
                                pp0[:, 0:512],
                                lnT[:, k * TS + t0 * 128: k * TS + (t0 + 1) * 128],
                                wvb[sl][:], start=(k == 0), stop=(k == NF - 1)),
                                wait=[ev_w] + (pp0rw.wdep() if k == 0 else []),
                                inc=(k == NF - 1))
                            ev_m1 = P.op("pe", lambda t, k=k, sl=sl, t1=t1: t.matmul(
                                pp1[:, 0:512],
                                lnT[:, k * TS + t1 * 128: k * TS + (t1 + 1) * 128],
                                wvb[sl][:], start=(k == 0), stop=(k == NF - 1)),
                                wait=(pp1rw.wdep() if k == 0 else []),
                                inc=True)
                            wvbrw[sl].read(ev_m1)
                        pp0rw.wrote(ev_m0)
                        pp1rw.wrote(ev_m1)
                        ev_v0 = P.op("ve", lambda v, t0=t0, pp0=pp0: v.tensor_add(
                            v_sb[:, (t0 + 1) * 512:(t0 + 2) * 512], bv_sb[:],
                            pp0[:, 0:512]),
                            wait=[ev_m0, ev_ld], inc=True)
                        pp0rw.read(ev_v0)
                        ev_vdone = P.op("ve", lambda v, t1=t1, pp1=pp1: v.tensor_add(
                            v_sb[:, (t1 + 1) * 512:(t1 + 2) * 512], bv_sb[:],
                            pp1[:, 0:512]),
                            wait=[ev_m1], inc=True)
                        pp1rw.read(ev_vdone)
                with ExitStack() as c2:
                    C2 = c2.enter_context
                    qf = [C2(nc.sbuf_tensor(f"qf{i}", [128, 512], F32))
                          for i in range(2)]
                    qt1 = C2(nc.sbuf_tensor("qt1", [128, 512], F32))
                    qt2 = C2(nc.sbuf_tensor("qt2", [128, 512], F32))
                    qfrw = [RW("qf0"), RW("qf1")]
                    ev_qrot = None
                    for m in range(NMQ):
                        wsl = m % 3
                        ev_w = P.op("gp", lambda gg, m=m, wsl=wsl: gg.dma_start(
                            out=wblk[wsl][:], in_=d_wq[m]),
                            wait=wblkrw[wsl].wdep(), inc=True)
                        wblkrw[wsl].wrote(ev_w)
                        ev_m = None
                        for grp in range(2):
                            g0 = grp * 512
                            fsl = grp % 2
                            pp, pprw = pool[grp % 2]
                            ps, psrw = pool[2 + grp % 2]
                            for k in range(NF):
                                ev_m = P.op("pe", lambda t, k=k, wsl=wsl, g0=g0,
                                            pp=pp: t.matmul(
                                                pp[:, 0:512],
                                                wblk[wsl][:, k * 128:(k + 1) * 128],
                                                lnT[:, k * TS + g0:k * TS + g0 + 512],
                                                start=(k == 0), stop=(k == NF - 1)),
                                            wait=(wblkrw[wsl].rdep() + pprw.wdep())
                                            if k == 0 else [],
                                            inc=(k == NF - 1))
                            pprw.wrote(ev_m)
                            ev_f = P.op("se", lambda s, fsl=fsl, m=m, pp=pp:
                                        s.activation(qf[fsl][:], pp[:, 0:512],
                                                     AF.Identity,
                                                     bias=bq_sb[:, m:m + 1]),
                                        wait=[ev_m] + qfrw[fsl].wdep(), inc=True)
                            pprw.read(ev_f)
                            qfrw[fsl].wrote(ev_f)
                            ev_sw = P.op("pe", lambda t, fsl=fsl, ps=ps: t.matmul(
                                ps[:, 0:512], swp_f[:], qf[fsl][:],
                                start=True, stop=True),
                                wait=[ev_f] + psrw.wdep(), inc=True)
                            psrw.wrote(ev_sw)
                            P.op("ve", lambda v, g0=g0, ps=ps: v.tensor_mul(
                                qt1[:], stab[:, g0:g0 + 512], ps[:, 0:512]),
                                wait=[ev_sw])
                            ev_r1 = P.op("ve", lambda v, fsl=fsl, g0=g0: v.tensor_mul(
                                qt2[:], qf[fsl][:], ctab[:, g0:g0 + 512]), inc=True)
                            psrw.read(ev_r1)
                            qfrw[fsl].read(ev_r1)
                            ev_qrot = P.op("ve", lambda v, m=m, g0=g0: v.tensor_add(
                                qrot[:, m * OWN + g0:m * OWN + g0 + 512],
                                qt1[:], qt2[:]), inc=True)
                        wblkrw[wsl].read(ev_m)

                ev_kv_local = max(ev_krot, ev_vdone,
                                  key=lambda e: e[1] if e else 0)
                ev_pk = None
                for edge, tok0 in ((0, 0), (1, OWN - 128)):
                    for mb in range(NMK):
                        ev_pk = P.op("sy", lambda ss, edge=edge, tok0=tok0, mb=mb:
                                     ss.dma_start(
                                         out=d_ccin[edge, :, mb * 128:(mb + 1) * 128],
                                         in_=krot[:, mb * EXT + HALO + tok0:
                                                  mb * EXT + HALO + tok0 + 128]),
                                     wait=[ev_krot], inc=True)
                    ev_pk = P.op("sy", lambda ss, edge=edge, tok0=tok0:
                                 ss.dma_start(
                                     out=d_ccin[edge, :, 512:1024],
                                     in_=v_sb[:, (1 + tok0 // 128) * 512:
                                              (2 + tok0 // 128) * 512]),
                                 wait=[ev_vdone], inc=True)
                P.op("gp", lambda g: g.collective_compute(
                    "AllGather", mybir.AluOpType.bypass,
                    replica_groups=[list(range(NCORE))],
                    ins=[d_ccin.opt()], outs=[d_ccout.opt()]).then_inc(cc_sem),
                    wait=[ev_pk])
                ev_g0 = P.op("gp", lambda g: g.indirect_dma_start(
                    halo_sb[:, 0:1024], None,
                    d_ccout.rearrange("r p c -> (r p) c"),
                    IndirectOffsetOnAxis(hidx_sb[:, 0:1], 0)),
                    wait=[("cc", 1), ev_ld], inc=True)
                ev_g1 = P.op("gp", lambda g: g.indirect_dma_start(
                    halo_sb[:, 1024:2048], None,
                    d_ccout.rearrange("r p c -> (r p) c"),
                    IndirectOffsetOnAxis(hidx_sb[:, 1:2], 0)), inc=True)
                for mb in range(NMK):
                    P.op("ve", lambda v, mb=mb: v.tensor_copy(
                        krot[:, mb * EXT: mb * EXT + 128],
                        halo_sb[:, mb * 128:(mb + 1) * 128]),
                        wait=[ev_g0] if mb == 0 else [])
                    P.op("ve", lambda v, mb=mb: v.tensor_copy(
                        krot[:, mb * EXT + HALO + OWN: (mb + 1) * EXT],
                        halo_sb[:, 1024 + mb * 128: 1024 + (mb + 1) * 128]),
                        wait=[ev_g1] if mb == 0 else [])
                P.op("ve", lambda v: v.tensor_copy(
                    v_sb[:, 0:512], halo_sb[:, 512:1024]))
                ev_halo = P.op("ve", lambda v: v.tensor_copy(
                    v_sb[:, (NTT - 1) * 512: NTT * 512],
                    halo_sb[:, 1536:2048]), inc=True)
                

                if "krot" in taps:
                    tap("krot", krot, [128, NMK * EXT], [ev_krot])
                if "v_sb" in taps:
                    tap("v_sb", v_sb, [128, NTT * 512], [ev_vdone])
                P.barrier()

                with ExitStack() as c2:
                    C2 = c2.enter_context
                    aout = C2(nc.sbuf_tensor("aout", [128, NMQ * QG], BF16))
                    pbuf = [C2(nc.sbuf_tensor(f"pbuf{i}", [128, KW], BF16))
                            for i in range(8)]
                    ptb = [C2(nc.sbuf_tensor(f"ptb{i}", [128, KW], BF16))
                           for i in range(4)]
                    nm4 = C2(nc.sbuf_tensor("anm4", [128, 4], F32))
                    ds4 = C2(nc.sbuf_tensor("ads4", [128, 4], F32))
                    es4 = C2(nc.sbuf_tensor("aes4", [128, 4], F32))
                    diag = [C2(nc.sbuf_tensor(f"adg{i}", [128, 128], BF16))
                            for i in range(8)]
                    pbrw = [RW(f"pb{i}") for i in range(8)]
                    ptrw = [RW(f"pt{i}") for i in range(4)]
                    dgrw = [RW(f"dg{i}") for i in range(8)]
                    for g in (1, 2, 0, 3):
                        g0 = g * QG
                        pav, pavrw = pool[3]
                        ev_ao = None
                        SKEW = 5
                        st_a = {}
                        st_b = {}
                        quad_prev = [None, None]
                        exp_prev = [None] * 4

                        def warm(k):
                            for _ in range(k):
                                P.op("pe", lambda t: t.matmul(
                                    pool[2][0][:, 0:128], id_f[:], id_f[:],
                                    start=True, stop=True))

                        def stage_a(idx):
                            qq, hh = idx // H, idx % H
                            qt = 2 * g + qq
                            mvar = 1 if qt == 0 else (2 if qt == NTT - 3 else 0)
                            ph, sub = hh // 2, hh % 2
                            gg_, i_ = ph // 4, ph % 4
                            h = gg_ * 8 + sub * 4 + i_
                            kvg = 2 * gg_ + sub
                            rbase = sub * 64
                            sl = idx % 4
                            slb = idx % 8
                            psc, pscrw = pool[4 + sl]
                            warm(4)
                            ev_sc = P.op("pe", lambda t, ph=ph, rbase=rbase,
                                         kvg=kvg, qt=qt, psc=psc: t.matmul(
                                             psc[:, 0:KW],
                                             qrot[rbase:rbase + 64,
                                                  ph * OWN + qt * 128:
                                                  ph * OWN + qt * 128 + 128],
                                             krot[rbase:rbase + 64,
                                                  (kvg // 2) * EXT + qt * 128:
                                                  (kvg // 2) * EXT + qt * 128 + KW],
                                             start=True, stop=True),
                                         wait=[ev_qrot,
                                               ev_halo if qt in (0, NTT - 3)
                                               else ev_kv_local]
                                         + pscrw.wdep(),
                                         inc=True)
                            pscrw.wrote(ev_sc)
                            ev_mk = P.op("ve", lambda v, mvar=mvar, psc=psc:
                                         v.tensor_tensor(
                                             psc.ap().rearrange(
                                                 "p (b c) -> p b c", c=128)[:, 0:3:2, :],
                                             msk.ap().rearrange(
                                                 "p (v b c) -> p v b c", v=3, c=128)
                                             [:, mvar, :, :],
                                             psc.ap().rearrange(
                                                 "p (b c) -> p b c", c=128)[:, 0:3:2, :],
                                             ALU.add),
                                         wait=[ev_sc, ev_ld], inc=True)
                            pscrw.wrote(ev_mk)
                            ev_rm = P.op("ve", lambda v, sl=sl, psc=psc: v.reduce_max(
                                nm4[:, sl:sl + 1], psc[:, 0:KW],
                                axis=AX.X, negate=True),
                                wait=[ev_mk, exp_prev[sl]], inc=True)
                            pscrw.read(ev_rm)
                            ev_ex = P.op("se", lambda s, sl=sl, slb=slb, psc=psc:
                                         s.activation(
                                             pbuf[slb][:], psc[:, 0:KW], AF.Exp,
                                             bias=nm4[:, sl:sl + 1],
                                             accum_out=ds4[:, sl:sl + 1]),
                                         wait=[ev_rm, quad_prev[0]]
                                         + pbrw[slb].wdep(), inc=True)
                            pscrw.read(ev_ex)
                            pbrw[slb].wrote(ev_ex)
                            exp_prev[sl] = ev_ex
                            dg = idx % 8
                            st_a[idx] = (ev_ex, kvg, qt, slb, dg, ph, rbase)
                            if sl == 3:
                                hh0 = hh - 3
                                ev_nm4 = P.op("ve", lambda v, hh0=hh0: v.tensor_add(
                                    es4[:], snko[:, hh0:hh0 + 4], nm4[:]),
                                    wait=[ev_rm, quad_prev[1]], inc=True)
                                ev_es4 = P.op("se", lambda s: s.activation(
                                    es4[:], es4[:], AF.Exp),
                                    wait=[ev_nm4, ev_ex], inc=True)
                                P.op("ve", lambda v: v.tensor_add(
                                    ds4[:], ds4[:], es4[:]), wait=[ev_es4])
                                ev_rc4 = P.op("ve", lambda v: v.reciprocal(
                                    ds4[:], ds4[:]), inc=True)
                                ev_dgl = None
                                for j in range(4):
                                    dgj = (idx - 3 + j) % 8
                                    ev_dgl = P.op("ve", lambda v, j=j, dgj=dgj:
                                                  v.tensor_scalar_mul(
                                                      diag[dgj][:], id_f[:],
                                                      ds4[:, j:j + 1]),
                                                  wait=[ev_rc4] + dgrw[dgj].wdep(),
                                                  inc=True)
                                    dgrw[dgj].wrote(ev_dgl)
                                    st_a[idx - 3 + j] += (ev_dgl,)
                                quad_prev[0] = ev_dgl
                                quad_prev[1] = ev_es4

                        def stage_b(idx):
                            ev_ex, kvg, qt, slb, dg, ph, rbase, ev_dg = st_a.pop(idx)
                            sl = idx % 4
                            ptp, ptprw = pool[idx % 2]
                            ev_tp = None
                            for jj in range(3):
                                ev_tp = P.op("pe", lambda t, slb=slb, jj=jj, ptp=ptp,
                                             dg=dg: t.matmul(
                                                 ptp[:, jj * 128:(jj + 1) * 128],
                                                 pbuf[slb][:, jj * 128:(jj + 1) * 128],
                                                 diag[dg][:], start=True, stop=True),
                                             wait=([ev_ex, ev_dg] + ptprw.wdep())
                                             if jj == 0 else [],
                                             inc=(jj == 2))
                            dgrw[dg].read(ev_tp)
                            ptprw.wrote(ev_tp)
                            pbrw[slb].read(ev_tp)
                            ev_pt = P.op("se", lambda ss, sl=sl, ptp=ptp:
                                         ss.copy(ptb[sl][:], ptp[:, 0:KW]),
                                         wait=[ev_tp] + ptrw[sl].wdep(), inc=True)
                            ptprw.read(ev_pt)
                            ptrw[sl].wrote(ev_pt)
                            st_b[idx] = (ev_pt, kvg, qt, sl, ph, rbase)

                        def stage_c(idx):
                            nonlocal ev_ao
                            ev_pt, kvg, qt, sl, ph, rbase = st_b.pop(idx)
                            qq = idx // H
                            rb = rbase
                            ev_av = None
                            warm(2)
                            for jj in range(3):
                                ev_av = P.op("pe", lambda t, jj=jj, qt=qt, kvg=kvg,
                                             sl=sl, rb=rb: t.matmul(
                                                 pav[rb:rb + 64, 0:128],
                                                 v_sb[:, (qt + jj) * 512 + kvg * 64:
                                                      (qt + jj) * 512 + kvg * 64 + 64],
                                                 ptb[sl][:, jj * 128:(jj + 1) * 128],
                                                 start=(jj == 0), stop=(jj == 2)),
                                             wait=([ev_pt,
                                                    ev_halo if qt in (0, NTT - 3)
                                                    else ev_kv_local]
                                                   + (pavrw.wdep() if rb == 0
                                                      else []))
                                             if jj == 0 else [],
                                             inc=(jj == 2))
                            ptrw[sl].read(ev_av)
                            if rb == 0:
                                pavrw.wrote(ev_av)
                            else:
                                ev_ao = P.op("se", lambda ss, ph=ph, qq=qq, pav=pav:
                                             ss.copy(
                                                 aout[:, ph * QG + qq * 128:
                                                      ph * QG + qq * 128 + 128],
                                                 pav[:, 0:128]),
                                             wait=[ev_av], inc=True)
                                pavrw.read(ev_ao)

                        SKEW2 = 3
                        for idx in range(2 * H + SKEW + SKEW2):
                            if idx < 2 * H:
                                stage_a(idx)
                            if SKEW <= idx < 2 * H + SKEW:
                                stage_b(idx - SKEW)
                            if idx >= SKEW + SKEW2:
                                stage_c(idx - SKEW - SKEW2)
                        ev_aout = ev_ao
                        if "aout" in taps and g == 0:
                            tap("aout", aout, [128, NMQ * QG], [ev_aout])
                        for mf in range(NMQ):
                            sl = mf % 3
                            ev_w = P.op("gp", lambda gg, mf=mf, sl=sl: gg.dma_start(
                                out=wblk[sl][:], in_=d_wo[mf]),
                                wait=wblkrw[sl].wdep(), inc=True)
                            wblkrw[sl].wrote(ev_w)
                            pp, pprw = pool[2 + mf % 2]
                            ev_m = None
                            for k in range(NMQ):
                                ev_m = P.op("pe", lambda t, k=k, sl=sl, pp=pp:
                                            t.matmul(
                                                pp[:, 0:QG],
                                                wblk[sl][:, k * 128:(k + 1) * 128],
                                                aout[:, k * QG:(k + 1) * QG],
                                                start=(k == 0), stop=(k == NMQ - 1)),
                                            wait=(wblkrw[sl].rdep() + pprw.wdep()
                                                  + [ev_aout]) if k == 0 else [],
                                            inc=(k == NMQ - 1))
                            pprw.wrote(ev_m)
                            wblkrw[sl].read(ev_m)
                            ev_x = P.op("ve", lambda v, mf=mf, g0=g0, pp=pp:
                                        v.scalar_tensor_tensor(
                                            x_sb[:, mf * TS + g0: mf * TS + g0 + QG],
                                            pp[:, 0:QG], bo_sb[:, mf:mf + 1],
                                            x_sb[:, mf * TS + g0: mf * TS + g0 + QG],
                                            op0=ALU.add, op1=ALU.add),
                                        wait=[ev_m], inc=True)
                            pprw.read(ev_x)
                if "x2" in taps:
                    P.barrier()
                    tap("x2", x_sb, [128, NF * TS], [])
                P.barrier()

        if upto == "all":
            with ExitStack() as ctx:
                emit_ffn(ctx, "f2", 0, OWN)
                P.barrier()

        with ExitStack() as ctx:
            pool = alloc_psum_pool(ctx, "out")
            ob = [ctx.enter_context(nc.sbuf_tensor(f"ob{i}", [128, DM], F32))
                  for i in range(2)]
            obrw = [RW("ob0"), RW("ob1")]
            ev_st = None
            for qt in range(8):
                sl = qt % 2
                ev_e = None
                for f in range(NF):
                    pp, pprw = pool[f % 4]
                    ev_t = P.op("pe", lambda t, f=f, qt=qt, pp=pp: t.transpose(
                        pp[:, 0:128],
                        x_sb[:, f * TS + qt * 128:
                             f * TS + (qt + 1) * 128],
                        id_f[:]),
                        wait=pprw.wdep(), inc=True)
                    pprw.wrote(ev_t)
                    ev_e = P.op("ve", lambda v, f=f, sl=sl, pp=pp: v.tensor_copy(
                        ob[sl][:, f * 128:(f + 1) * 128], pp[:, 0:128]),
                        wait=[ev_t] + (obrw[sl].wdep() if f == 0 else []), inc=True)
                    pprw.read(ev_e)
                obrw[sl].wrote(ev_e)
                ev_st = P.op("sy", lambda s, qt=qt, sl=sl: s.dma_start(
                    out=d_y[qt * 128:(qt + 1) * 128, :], in_=ob[sl][:]),
                    wait=[ev_e], inc=True)
                obrw[sl].read(ev_st)
            P.op("ve", None, wait=[ev_st])

        with nc.Block() as block:
            @block.tensor
            def _(t):
                P.emit(t, "pe", sems)

            @block.vector
            def _(v):
                P.emit(v, "ve", sems)

            @block.scalar
            def _(s):
                P.emit(s, "se", sems)

            @block.sync
            def _(s):
                P.emit(s, "sy", sems)

            @block.gpsimd
            def _(g):
                P.emit(g, "gp", sems)

    return nc, d_taps


def _retile_lhsT(W):
    K, M = W.shape
    a = W.reshape(K // 128, 128, M // 128, 128).transpose(2, 1, 0, 3)
    return np.ascontiguousarray(a.reshape(M // 128, 128, K))


def prep_inputs(L0_attn_hidden_states, L0_attn_Wq, L0_attn_bq, L0_attn_Wk,
                L0_attn_bk, L0_attn_Wv, L0_attn_bv, L0_attn_Wo, L0_attn_bo,
                L0_attn_sinks, L0_ffn_gate_w, L0_ffn_gate_b, L0_ffn_expert_bias,
                L0_ffn_w_gate_up, L0_ffn_w_down):
    f32 = np.float32
    x = np.asarray(L0_attn_hidden_states, f32).reshape(S, DM)
    xT = np.ascontiguousarray(x.T)

    wgu = np.asarray(L0_ffn_w_gate_up, f32)
    a = wgu.reshape(NE, 2, NPAIR, 128, NF, 128).transpose(0, 2, 1, 5, 4, 3)
    arr_gu = np.ascontiguousarray(a.reshape(NE, GUBLK, 128, 2048))

    wdn = np.asarray(L0_ffn_w_down, f32)
    arr_wd = np.ascontiguousarray(
        wdn.reshape(NE, NDM, 128, NPAIR, 128).transpose(0, 1, 4, 3, 2)
        .reshape(NE, NDM, 128, 2816))

    hd_perm = np.empty(H * D, dtype=np.int64)
    for h in range(H):
        tile_ = (h // 8) * 4 + (h % 4)
        base_ = ((h // 4) % 2) * 64
        hd_perm[h * D:(h + 1) * D] = tile_ * 128 + base_ + np.arange(D)
    wq_flat = np.asarray(L0_attn_Wq, f32).reshape(DM, H * D)
    wq_p = np.empty_like(wq_flat)
    wq_p[:, hd_perm] = wq_flat
    wo_flat = np.asarray(L0_attn_Wo, f32).reshape(H * D, DM)
    wo_p = np.empty_like(wo_flat)
    wo_p[hd_perm, :] = wo_flat
    wq_a = _retile_lhsT(wq_p)
    wk_a = _retile_lhsT(np.asarray(L0_attn_Wk, f32).reshape(DM, HKV * D))
    wo_a = _retile_lhsT(wo_p)
    wv_n = np.ascontiguousarray(np.asarray(L0_attn_Wv, f32).reshape(DM, HKV * D))

    gw = np.asarray(L0_ffn_gate_w, f32)
    gw_a = np.ascontiguousarray(gw.T.reshape(NF, 128, NE).transpose(1, 0, 2)
                                .reshape(128, NF * NE))

    def tile_col(b):
        return np.ascontiguousarray(np.asarray(b, f32).reshape(-1, 128).T)

    bq_flat = np.asarray(L0_attn_bq, f32).reshape(H * D)
    bq_p = np.empty_like(bq_flat)
    bq_p[hd_perm] = bq_flat
    bq_t = tile_col(bq_p)
    bk_t = tile_col(np.asarray(L0_attn_bk, f32).reshape(HKV * D))
    bo_t = tile_col(np.asarray(L0_attn_bo, f32).reshape(DM))
    bv_bc = np.ascontiguousarray(np.broadcast_to(
        np.asarray(L0_attn_bv, f32).reshape(1, HKV * D), (128, HKV * D)))
    gb_bc = np.ascontiguousarray(np.broadcast_to(
        np.asarray(L0_ffn_gate_b, f32).reshape(1, NE), (128, NE)))
    eb_bc = np.ascontiguousarray(np.broadcast_to(
        np.asarray(L0_ffn_expert_bias, f32).reshape(1, NE), (128, NE)))
    snk = np.ascontiguousarray(np.broadcast_to(
        np.asarray(L0_attn_sinks, f32).reshape(1, H), (128, H)))
    snkn = np.ascontiguousarray(-snk)
    hh_order = []
    for hh in range(H):
        ph, sub = hh // 2, hh % 2
        hh_order.append((ph // 4) * 8 + sub * 4 + (ph % 4))
    snko = np.ascontiguousarray(snk[:, hh_order])

    id_f = np.eye(128, dtype=f32)
    sel = np.zeros((NE, NE * 128), dtype=f32)
    for e in range(NE):
        sel[e, e * 128:(e + 1) * 128] = 1.0
    swp = np.zeros((128, 128), dtype=f32)
    for i in range(0, 128, 2):
        swp[i, i + 1] = 1.0
        swp[i + 1, i] = 1.0

    inv_freq = (1.0 / (WL ** (np.arange(D // 2, dtype=f32) * 2.0 / D))).astype(f32)
    per_core = []
    for c in range(NCORE):
        lo = c * OWN - HALO
        xT_own = np.ascontiguousarray(xT[:, c * OWN:(c + 1) * OWN])

        gpos = (c * OWN + np.arange(OWN)).astype(f32)
        ang = gpos[None, :] * inv_freq[:, None]
        cos32 = np.cos(ang).astype(f32) * RSCALE
        sin32 = np.sin(ang).astype(f32) * RSCALE
        dd = np.arange(128)
        ii = (dd % 64) // 2
        sgn = np.where(dd % 2 == 0, -1.0, 1.0).astype(f32)
        ctab = np.ascontiguousarray(cos32[ii])
        stab = np.ascontiguousarray(sin32[ii] * sgn[:, None])

        qi = np.arange(128)[:, None]
        j = np.arange(KW)[None, :]
        band = (j - qi >= 0) & (j - qi <= 2 * SW)
        v0 = np.where(band, 0.0, NEG).astype(f32)
        if c == 0:
            left_ok = np.broadcast_to((lo + j) >= 0, band.shape)
            v1 = np.where(band & left_ok, 0.0, NEG).astype(f32)
        else:
            v1 = v0
        if c == NCORE - 1:
            right_ok = np.broadcast_to((lo + (NTT - 3) * 128 + j) < S, band.shape)
            v2 = np.where(band & right_ok, 0.0, NEG).astype(f32)
        else:
            v2 = v0
        msk = np.ascontiguousarray(np.concatenate(
            [np.concatenate([v[:, 0:128], v[:, 256:KW]], axis=1)
             for v in (v0, v1, v2)], axis=1))
        pp_ = np.arange(128, dtype=np.int32)
        left_slab = ((c - 1) % NCORE) * 2 + 1
        right_slab = ((c + 1) % NCORE) * 2 + 0
        hidx = np.stack([left_slab * 128 + pp_, right_slab * 128 + pp_],
                        axis=1).astype(np.int32)
        per_core.append({"xT": xT_own, "ctab": ctab, "stab": stab, "msk": msk,
                         "hidx": hidx})

    shared = {
        "wgu": arr_gu, "wdn": arr_wd, "wq": wq_a, "wk": wk_a, "wv": wv_n,
        "wo": wo_a, "gw": gw_a, "bq": bq_t, "bk": bk_t, "bv": bv_bc, "bo": bo_t,
        "gb": gb_bc, "eb": eb_bc, "snk": snk, "snkn": snkn, "snko": snko, "idf": id_f,
        "swp": swp, "sel": sel,
    }
    in_maps = []
    for c in range(NCORE):
        m = dict(shared)
        m.update(per_core[c])
        in_maps.append(m)
    return in_maps


def kernel(**inputs):
    in_maps = prep_inputs(**inputs)
    nc, _ = build_core_program(upto="all", taps=())
    res = run_bass_kernel_spmd(nc, in_maps, core_ids=list(range(NCORE)))
    y = np.concatenate([r["y"] for r in res.results], axis=0)
    return y.reshape(1, S, DM)


# version 19 (final)
# speedup vs baseline: 1.0307x; 1.0182x over previous
import numpy as np
import ml_dtypes
from contextlib import ExitStack
import concourse.bass as bass
import concourse.mybir as mybir
from concourse.bass_utils import run_bass_kernel_spmd
from concourse.bass_core_rust import IndirectOffsetOnAxis

F32 = mybir.dt.float32
BF16 = mybir.dt.bfloat16
AF = mybir.ActivationFunctionType
ALU = mybir.AluOpType
AX = mybir.AxisListType

NCORE = 8
S = 8192
DM = 2048
OWN = 1024
HALO = 128
EXT = OWN + 2 * HALO
NF = DM // 128
NE = 8
INTER = 2816
NPAIR = INTER // 128
GUBLK = 2 * NPAIR
NDM = DM // 128
H, HKV, D = 32, 8, 64
NMQ, NMK = 16, 4
SW = 128
WL = 150000.0
RSCALE = float(D) ** -0.25
NEG = -1.0e9
EPS = 1e-5
NTT = EXT // 128
TS = OWN
QG = 256
NQG = OWN // QG
KW = 384

DMA_Q = ("sy", "gp")
ENGS = ("pe", "ve", "se", "sy", "gp")


def chunk_list(t0, tn, width=512):
    out = []
    o = t0
    while o < t0 + tn:
        w = min(width, t0 + tn - o)
        out.append((o, w))
        o += w
    return out


class Prog:

    def __init__(self):
        self.streams = {e: [] for e in ENGS}
        self.count = {e: 0 for e in ENGS}
        self.mark = {}

    def op(self, eng, fn, wait=(), inc=False):
        waits = []
        for evt in wait:
            if evt is None:
                continue
            p, v = evt
            if p == eng and eng == "pe":
                continue
            k = (eng, p)
            if self.mark.get(k, 0) < v:
                self.mark[k] = v
                waits.append((p, v))
        ev = None
        if inc:
            self.count[eng] += 16 if eng in DMA_Q else 1
            ev = (eng, self.count[eng])
        self.streams[eng].append((tuple(waits), fn, inc))
        return ev

    def barrier(self):
        evs = [(e, self.count[e]) for e in ENGS if self.count[e] > 0]
        for e in ENGS:
            for p, v in evs:
                if p == e:
                    continue
                k = (e, p)
                if self.mark.get(k, 0) < v:
                    self.mark[k] = v
                    self.streams[e].append((((p, v),), None, False))

    def emit(self, eng_obj, eng_name, sems):
        step = 16 if eng_name in DMA_Q else 1
        for waits, fn, inc in self.streams[eng_name]:
            for p, v in waits:
                eng_obj.wait_ge(sems[p], v)
            if fn is None:
                continue
            ins = fn(eng_obj)
            if inc:
                ins.then_inc(sems[eng_name], step)


class RW:

    def __init__(self, name=""):
        self.name = name
        self.wr = None
        self.rds = []

    def rdep(self):
        return [self.wr] if self.wr is not None else []

    def wdep(self):
        d = list(self.rds)
        if self.wr is not None:
            d.append(self.wr)
        return d

    def wrote(self, ev):
        assert ev is not None, f"RW {self.name}: writer must inc"
        self.wr = ev
        self.rds = []

    def read(self, ev):
        assert ev is not None, f"RW {self.name}: reader must inc"
        self.rds.append(ev)


def build_core_program(upto="all", taps=()):
    nc = bass.Bass("TRN2", target_bir_lowering=False)

    d_x = nc.dram_tensor("xT", [DM, TS], F32, kind="ExternalInput").ap()
    d_gu = nc.dram_tensor("wgu", [NE, GUBLK, 128, 2048], F32, kind="ExternalInput").ap()
    d_wd = nc.dram_tensor("wdn", [NE, NDM, 128, 2816], F32, kind="ExternalInput").ap()
    d_wq = nc.dram_tensor("wq", [NMQ, 128, 2048], F32, kind="ExternalInput").ap()
    d_wk = nc.dram_tensor("wk", [NMK, 128, 2048], F32, kind="ExternalInput").ap()
    d_wv = nc.dram_tensor("wv", [DM, 512], F32, kind="ExternalInput").ap()
    d_wo = nc.dram_tensor("wo", [NMQ, 128, 2048], F32, kind="ExternalInput").ap()
    d_gw = nc.dram_tensor("gw", [128, NF * NE], F32, kind="ExternalInput").ap()
    d_bq = nc.dram_tensor("bq", [128, NMQ], F32, kind="ExternalInput").ap()
    d_bk = nc.dram_tensor("bk", [128, NMK], F32, kind="ExternalInput").ap()
    d_bv = nc.dram_tensor("bv", [128, 512], F32, kind="ExternalInput").ap()
    d_bo = nc.dram_tensor("bo", [128, NMQ], F32, kind="ExternalInput").ap()
    d_gb = nc.dram_tensor("gb", [128, NE], F32, kind="ExternalInput").ap()
    d_eb = nc.dram_tensor("eb", [128, NE], F32, kind="ExternalInput").ap()
    d_snk = nc.dram_tensor("snk", [128, H], F32, kind="ExternalInput").ap()
    d_snko = nc.dram_tensor("snko", [128, H], F32, kind="ExternalInput").ap()
    d_snkn = nc.dram_tensor("snkn", [128, H], F32, kind="ExternalInput").ap()
    d_cos = nc.dram_tensor("ctab", [128, TS], F32, kind="ExternalInput").ap()
    d_sin = nc.dram_tensor("stab", [128, TS], F32, kind="ExternalInput").ap()
    d_msk = nc.dram_tensor("msk", [128, 3 * 256], F32, kind="ExternalInput").ap()
    d_idf = nc.dram_tensor("idf", [128, 128], F32, kind="ExternalInput").ap()
    d_swp = nc.dram_tensor("swp", [128, 128], F32, kind="ExternalInput").ap()
    d_sel = nc.dram_tensor("sel", [NE, NE * 128], F32, kind="ExternalInput").ap()
    d_hidx = nc.dram_tensor("hidx", [128, 2], mybir.dt.int32,
                            kind="ExternalInput").ap()
    d_ccin = nc.dram_tensor("cc_in", [2, 128, 1024], BF16, kind="Internal").ap()
    d_ccout = nc.dram_tensor("cc_out", [16, 128, 1024], BF16, kind="Internal").ap()
    d_y = nc.dram_tensor("y", [OWN, DM], F32, kind="ExternalOutput").ap()
    d_taps = {}

    P = Prog()

    with ExitStack() as outer:
        E = outer.enter_context
        sems = {e: E(nc.semaphore(f"sem_{e}")) for e in ENGS}

        x_sb = E(nc.sbuf_tensor("x_sb", [128, NF * TS], F32))
        ones_col = E(nc.sbuf_tensor("ones_col", [128, 1], BF16))
        ones_row_f = E(nc.sbuf_tensor("ones_row_f", [1, 128], F32))
        ones_row_b = E(nc.sbuf_tensor("ones_row_b", [1, 128], BF16))
        eps_t = E(nc.sbuf_tensor("eps_t", [1, 1], F32))
        inv_t = E(nc.sbuf_tensor("inv_t", [1, 1], F32))
        half_t = E(nc.sbuf_tensor("half_t", [128, 1], F32))
        id_f = E(nc.sbuf_tensor("id_f", [128, 128], F32))
        gb_bc = E(nc.sbuf_tensor("gb_bc", [128, NE], F32))
        eb_bc = E(nc.sbuf_tensor("eb_bc", [128, NE], F32))
        gw_bf = E(nc.sbuf_tensor("gw_bf", [128, NF * NE], BF16))
        sel_bf = E(nc.sbuf_tensor("sel_bf", [NE, NE * 128], BF16))

        ev_x0 = P.op("sy", lambda s: s.dma_start(
            out=x_sb.ap().rearrange("p (f t) -> p f t", f=NF),
            in_=d_x.rearrange("(f p) t -> p f t", p=128)), inc=True)
        ev_misc = None
        for dst, src in ((id_f, d_idf), (gb_bc, d_gb), (eb_bc, d_eb)):
            ev_misc = P.op("sy", lambda s, a=dst, b=src:
                           s.dma_start(out=a[:], in_=b), inc=True)
        P.op("gp", lambda g: g.dma_start(out=gw_bf[:], in_=d_gw), inc=True)
        P.op("gp", lambda g: g.dma_start(out=sel_bf[:], in_=d_sel), inc=True)
        for t, vv in ((ones_col, 1.0), (ones_row_f, 1.0), (ones_row_b, 1.0),
                      (eps_t, EPS), (inv_t, 1.0 / DM), (half_t, 0.5)):
            P.op("ve", lambda v, a=t, c=vv: v.memset(a[:], c), inc=True)
        P.barrier()

        def alloc_psum_pool(ctx, tag):
            return [(ctx.enter_context(nc.psum_tensor(f"ps_{tag}_{i}", [128, 512], F32)),
                     RW(f"ps_{tag}_{i}")) for i in range(8)]

        def emit_ln(ctx, tag, lnT, tok0, tokn, pool):
            ev_out = None
            ev_chunks = {}
            with ExitStack() as c2:
                C = c2.enter_context
                xb = [C(nc.sbuf_tensor(f"xb{tag}{i}", [128, 512], BF16)) for i in range(2)]
                sq = [C(nc.sbuf_tensor(f"sq{tag}{i}", [128, 512], BF16)) for i in range(2)]
                row_m = C(nc.sbuf_tensor(f"rm{tag}", [1, 512], F32))
                row_b = C(nc.sbuf_tensor(f"rb{tag}", [1, 512], F32))
                row_c = C(nc.sbuf_tensor(f"rc{tag}", [1, 512], F32))
                ln_tmp = C(nc.sbuf_tensor(f"lt{tag}", [128, 512], F32))
                xbrw = [RW("xb0"), RW("xb1")]
                pst, strw = pool[6]
                pss, ssrw = pool[5]
                pra, prarw = pool[0]
                prb, prbrw = pool[1]
                for (c0, cn) in chunk_list(tok0, tokn):
                    ev_mm = None
                    for k in range(NF):
                        sl = k % 2
                        xcol = x_sb[:, k * TS + c0: k * TS + c0 + cn]
                        evc = P.op("se", lambda s, a=xb[sl], b=xcol, n=cn:
                                   s.copy(a[:, 0:n], b),
                                   wait=xbrw[sl].wdep(), inc=True)
                        evs = P.op("se", lambda s, a=sq[sl], b=xcol, n=cn:
                                   s.activation(a[:, 0:n], b, AF.Square), inc=True)
                        P.op("pe", lambda t, a=xb[sl], n=cn, k=k:
                             t.matmul(pst[0:1, 0:n], ones_col[:], a[:, 0:n],
                                      start=(k == 0), stop=(k == NF - 1)),
                             wait=[evs] + ((strw.wdep() + ssrw.wdep())
                                           if k == 0 else []))
                        ev_mm = P.op("pe", lambda t, a=sq[sl], n=cn, k=k:
                                     t.matmul(pss[0:1, 0:n], ones_col[:], a[:, 0:n],
                                              start=(k == 0), stop=(k == NF - 1)),
                                     inc=True)
                        xbrw[sl].wrote(ev_mm)
                    strw.wrote(ev_mm)
                    ssrw.wrote(ev_mm)
                    ev_q1 = P.op("ve", lambda v, n=cn: v.tensor_scalar_mul(
                        row_m[0:1, 0:n], pst[0:1, 0:n], inv_t[0:1, 0:1]),
                        wait=[ev_mm], inc=True)
                    ev = P.op("ve", lambda v, n=cn: v.tensor_scalar_mul(
                        row_b[0:1, 0:n], pss[0:1, 0:n], inv_t[0:1, 0:1]), inc=True)
                    strw.read(ev)
                    ssrw.read(ev)
                    P.op("ve", lambda v, n=cn: v.tensor_mul(
                        row_c[0:1, 0:n], row_m[0:1, 0:n], row_m[0:1, 0:n]),
                        wait=[ev_q1, ev])
                    ev_var = P.op("ve", lambda v, n=cn: v.tensor_sub(
                        row_b[0:1, 0:n], row_b[0:1, 0:n], row_c[0:1, 0:n]), inc=True)
                    ev_sd = P.op("se", lambda s, n=cn: s.activation(
                        row_b[0:1, 0:n], row_b[0:1, 0:n], AF.Sqrt,
                        bias=eps_t[0:1, 0:1]), wait=[ev_var], inc=True)
                    ev_rcp = P.op("ve", lambda v, n=cn: v.reciprocal(
                        row_b[0:1, 0:n], row_b[0:1, 0:n]), wait=[ev_sd], inc=True)
                    ev_rows = P.op("ve", lambda v, n=cn: v.tensor_mul(
                        row_c[0:1, 0:n], row_m[0:1, 0:n], row_b[0:1, 0:n]),
                        wait=[ev_rcp], inc=True)
                    ev_b1 = P.op("pe", lambda t, n=cn: t.matmul(
                        pra[:, 0:n], ones_row_f[:], row_b[0:1, 0:n],
                        start=True, stop=True),
                        wait=[ev_rows] + prarw.wdep(), inc=True)
                    prarw.wrote(ev_b1)
                    ev_b2 = P.op("pe", lambda t, n=cn: t.matmul(
                        prb[:, 0:n], ones_row_f[:], row_c[0:1, 0:n],
                        start=True, stop=True), wait=prbrw.wdep(), inc=True)
                    prbrw.wrote(ev_b2)
                    ev_ln = None
                    for k in range(NF):
                        P.op("ve", lambda v, k=k, c0=c0, n=cn: v.tensor_mul(
                            ln_tmp[:, 0:n],
                            x_sb[:, k * TS + c0: k * TS + c0 + n], pra[:, 0:n]),
                            wait=[ev_b1, ev_b2] if k == 0 else [])
                        ev_ln = P.op("ve", lambda v, k=k, c0=c0, n=cn: v.tensor_sub(
                            lnT[:, k * TS + c0: k * TS + c0 + n],
                            ln_tmp[:, 0:n], prb[:, 0:n]),
                            inc=(k == NF - 1))
                    prarw.read(ev_ln)
                    prbrw.read(ev_ln)
                    ev_chunks[c0] = ev_ln
                    ev_out = ev_ln
            return ev_out, ev_chunks

        def emit_router(ctx, tag, lnT, comb_em, tok0, tokn, pool, ev_lnT_or):
            ph, phrw = pool[7]
            ev_ce = None
            with ExitStack() as c2:
                C = c2.enter_context
                r_tm = C(nc.sbuf_tensor(f"rtm{tag}", [128, NE], F32))
                r_t2 = C(nc.sbuf_tensor(f"rt2{tag}", [128, NE], F32))
                m8 = C(nc.sbuf_tensor(f"m8{tag}", [128, 8], F32))
                nmr = C(nc.sbuf_tensor(f"nmr{tag}", [128, 1], F32))
                dsr = C(nc.sbuf_tensor(f"dsr{tag}", [128, 1], F32))
                for ti in range(tok0 // 128, (tok0 + tokn) // 128):
                    ev_mm = None
                    for k in range(NF):
                        ev_mm = P.op("pe", lambda t, k=k, ti=ti: t.matmul(
                            ph[:, 0:NE],
                            lnT[:, k * TS + ti * 128: k * TS + (ti + 1) * 128],
                            gw_bf[:, k * NE:(k + 1) * NE],
                            start=(k == 0), stop=(k == NF - 1)),
                            wait=([ev_lnT_or(ti)] + phrw.wdep()) if k == 0 else [],
                            inc=(k == NF - 1))
                    phrw.wrote(ev_mm)
                    ev1 = P.op("ve", lambda v: v.tensor_add(
                        r_tm[:], ph[:, 0:NE], gb_bc[:]), wait=[ev_mm], inc=True)
                    phrw.read(ev1)
                    if f"rlg{tag}" in taps and ti == tok0 // 128:
                        tap(f"rlg{tag}", r_tm, [128, NE], [ev1])
                    ev_nm = P.op("ve", lambda v: v.reduce_max(
                        nmr[:], r_tm[:], axis=AX.X, negate=True),
                        wait=[ev1], inc=True)
                    ev_ex = P.op("se", lambda s: s.activation(
                        r_tm[:], r_tm[:], AF.Exp, bias=nmr[:, 0:1],
                        accum_out=dsr[:]), wait=[ev_nm], inc=True)
                    ev_rc = P.op("ve", lambda v: v.reciprocal(dsr[:], dsr[:]),
                                 wait=[ev_ex], inc=True)
                    if f"rds{tag}" in taps and ti == tok0 // 128:
                        tap(f"rds{tag}", dsr, [128, 1], [ev_rc])
                        tap(f"rex{tag}", r_tm, [128, NE], [ev_rc])
                    ev_sc = P.op("ve", lambda v: v.tensor_scalar_mul(
                        r_tm[:], r_tm[:], dsr[:, 0:1]),
                        wait=[ev_rc], inc=True)
                    if f"rds{tag}" in taps and ti == tok0 // 128:
                        tap(f"rsc{tag}", r_tm, [128, NE], [ev_sc])
                    ev_eb = P.op("ve", lambda v: v.tensor_add(
                        r_tm[:], r_tm[:], eb_bc[:]), wait=[ev_sc], inc=True)
                    if f"rsm{tag}" in taps and ti == tok0 // 128:
                        tap(f"rsm{tag}", r_tm, [128, NE], [ev_eb])
                    ev_m8 = P.op("ve", lambda v: v.max(out=m8[:], in_=r_tm[:]),
                                 wait=[ev_eb], inc=True)
                    ev_st2 = P.op("ve", lambda v: v.scalar_tensor_tensor(
                        r_t2[:], r_tm[:], m8[:, 1:2], r_tm[:],
                        op0=ALU.is_ge, op1=ALU.mult), wait=[ev_m8], inc=True)
                    ev_cb = P.op("ve", lambda v: v.tensor_scalar_mul(
                        r_t2[:], r_t2[:], half_t[:, 0:1]), wait=[ev_st2], inc=True)
                    ev_tp = P.op("pe", lambda t: t.matmul(
                        ph[0:NE, 0:128], r_t2[:], id_f[:], start=True, stop=True),
                        wait=[ev_cb, ev_misc] + phrw.wdep(), inc=True)
                    phrw.wrote(ev_tp)
                    ev_ce = P.op("ve", lambda v, ti=ti: v.tensor_copy(
                        comb_em[0:NE, ti * 128:(ti + 1) * 128], ph[0:NE, 0:128]),
                        wait=[ev_tp], inc=True)
                    phrw.read(ev_ce)
            return ev_ce

        def tap(name, sbuf, shape, wait):
            dd = nc.dram_tensor("tap_" + name, shape, sbuf.dtype,
                                kind="ExternalOutput").ap()
            d_taps[name] = dd
            P.op("sy", lambda s: s.dma_start(out=dd,
                                             in_=sbuf[0:shape[0], 0:shape[1]]),
                 wait=wait, inc=True)

        def emit_ffn(ctx, tag, tok0, tokn, stage="full"):
            pool = alloc_psum_pool(ctx, tag)
            lnT = ctx.enter_context(nc.sbuf_tensor(f"lnT{tag}", [128, NF * TS], BF16))
            comb_em = ctx.enter_context(nc.sbuf_tensor(f"cem{tag}", [NE, TS], BF16))
            wgu_t = ctx.enter_context(
                nc.sbuf_tensor(f"wgu{tag}", [128, 4 * 2048], BF16))
            wgu = [wgu_t[:, i * 2048:(i + 1) * 2048] for i in range(4)]
            wdn_t = ctx.enter_context(
                nc.sbuf_tensor(f"wdn{tag}", [128, 2 * 2816], BF16))
            wdn = [wdn_t[:, i * 2816:(i + 1) * 2816] for i in range(2)]
            inter = ctx.enter_context(
                nc.sbuf_tensor(f"inter{tag}", [128, NPAIR * TS], BF16))
            sil = [ctx.enter_context(nc.sbuf_tensor(f"sil{tag}0", [128, 512], F32))]
            cbc = ctx.enter_context(nc.sbuf_tensor(f"cbc{tag}", [128, TS], BF16))

            ev_lnT, ev_lnc = emit_ln(ctx, tag, lnT, tok0, tokn, pool)

            def lnc(c0):
                ks = [k for k in ev_lnc if k <= c0]
                return ev_lnc[max(ks)] if ks else ev_lnT
            if f"lnT{tag}" in taps:
                tap(f"lnT{tag}", lnT, [128, NF * TS], [ev_lnT])
            if stage == "ln":
                return ev_lnT
            ev_comb = emit_router(ctx, tag, lnT, comb_em, tok0, tokn, pool,
                                  lambda ti: lnc(ti * 128))
            if f"cem{tag}" in taps:
                tap(f"cem{tag}", comb_em, [NE, TS], [ev_comb])
            if stage == "router":
                return ev_comb

            chunks = chunk_list(tok0, tokn)
            nch = len(chunks)
            wgurw = [RW(f"wgu{i}") for i in range(4)]
            wdnrw = [RW(f"wdn{i}") for i in range(2)]
            silrw = [RW("sil0")]
            cbcrw = RW("cbc")
            pc, pcrw = pool[7]

            def gu_dma(e, pair):
                sl0 = (2 * pair) % 4
                ev = P.op("gp", lambda g, e=e, pair=pair, sl0=sl0: g.dma_start(
                    out=wgu_t.ap().rearrange("p (b c) -> p b c", c=2048)
                    [:, sl0:sl0 + 2, :],
                    in_=d_gu[e, 2 * pair:2 * pair + 2].rearrange(
                        "b p c -> p b c")),
                    wait=wgurw[sl0].wdep() + wgurw[sl0 + 1].wdep(), inc=True)
                wgurw[sl0].wrote(ev)
                wgurw[sl0 + 1].wrote(ev)

            def wd_dma(e, m):
                sl = m % 2
                ev = P.op("gp", lambda g, e=e, m=m, sl=sl: g.dma_start(
                    out=wdn[sl][:], in_=d_wd[e, m]),
                    wait=wdnrw[sl].wdep(), inc=True)
                wdnrw[sl].wrote(ev)

            ev_x_last = None
            dn_ctr = 0
            for e in range(NE):
                for (c0, cn) in chunks:
                    ev = P.op("pe", lambda t, e=e, c0=c0, cn=cn: t.matmul(
                        pc[:, 0:cn], sel_bf[:, e * 128:(e + 1) * 128],
                        comb_em[0:NE, c0:c0 + cn],
                        start=True, stop=True),
                        wait=[ev_comb] + pcrw.wdep(), inc=True)
                    pcrw.wrote(ev)
                    ev = P.op("ve", lambda v, c0=c0, cn=cn: v.tensor_copy(
                        cbc[:, c0:c0 + cn], pc[:, 0:cn]),
                        wait=[ev] + (cbcrw.wdep() if c0 == chunks[0][0] else []),
                        inc=True)
                    pcrw.read(ev)
                ev_cbc = ev
                cbcrw.wrote(ev_cbc)
                gu_dma(e, 0)
                gu_dma(e, 1)
                ev_int_c = {}
                gu_ctr = 0
                for i in range(NPAIR):
                    gsl, usl = (2 * i) % 4, (2 * i + 1) % 4
                    pus, pgs = [], []
                    for (c0, cn) in chunks:
                        pu, purw = pool[gu_ctr % 6]
                        gu_ctr += 1
                        ev_u = None
                        for k in range(NF):
                            ev_u = P.op("pe", lambda t, k=k, usl=usl, c0=c0, cn=cn,
                                        pu=pu: t.matmul(
                                            pu[:, 0:cn],
                                            wgu[usl][:, k * 128:(k + 1) * 128],
                                            lnT[:, k * TS + c0: k * TS + c0 + cn],
                                            start=(k == 0), stop=(k == NF - 1)),
                                        wait=(wgurw[usl].rdep() + purw.wdep()
                                              + [lnc(c0)]) if k == 0 else [],
                                        inc=(k == NF - 1))
                        purw.wrote(ev_u)
                        pus.append((pu, purw, ev_u))
                    wgurw[usl].read(pus[-1][2])
                    for ci, (c0, cn) in enumerate(chunks):
                        pg, pgrw = pool[gu_ctr % 6]
                        gu_ctr += 1
                        ev_g = None
                        for k in range(NF):
                            ev_g = P.op("pe", lambda t, k=k, gsl=gsl, c0=c0, cn=cn,
                                        pg=pg: t.matmul(
                                            pg[:, 0:cn],
                                            wgu[gsl][:, k * 128:(k + 1) * 128],
                                            lnT[:, k * TS + c0: k * TS + c0 + cn],
                                            start=(k == 0), stop=(k == NF - 1)),
                                        wait=(wgurw[gsl].rdep() + pgrw.wdep())
                                        if k == 0 else [],
                                        inc=(k == NF - 1))
                        pgrw.wrote(ev_g)
                        pgs.append((pg, pgrw, ev_g))
                        ssl = 0
                        pu, purw, ev_u = pus[ci]
                        ev_s = P.op("se", lambda s, ssl=ssl, cn=cn, pg=pg:
                                    s.activation(sil[ssl][:, 0:cn], pg[:, 0:cn],
                                                 AF.Silu),
                                    wait=[ev_g] + silrw[ssl].wdep(), inc=True)
                        pgrw.read(ev_s)
                        silrw[ssl].wrote(ev_s)
                        ev_t = P.op("ve", lambda v, ssl=ssl, i=i, c0=c0, cn=cn,
                                    pu=pu: v.tensor_mul(
                                        inter[:, i * TS + c0: i * TS + c0 + cn],
                                        sil[ssl][:, 0:cn], pu[:, 0:cn]),
                                    wait=[ev_s, ev_u], inc=True)
                        purw.read(ev_t)
                        silrw[ssl].read(ev_t)
                        ev_i = P.op("ve", lambda v, i=i, c0=c0, cn=cn: v.tensor_mul(
                            inter[:, i * TS + c0: i * TS + c0 + cn],
                            inter[:, i * TS + c0: i * TS + c0 + cn],
                            cbc[:, c0:c0 + cn]),
                            wait=[ev_cbc], inc=True)
                        cbcrw.read(ev_i)
                        ev_int_c[ci] = ev_i
                    wgurw[gsl].read(pgs[-1][2])
                    if i + 2 < NPAIR:
                        gu_dma(e, i + 2)
                wd_dma(e, 0)
                wd_dma(e, 1)
                for m in range(NDM):
                    bset = [4, 5, 6] if m % 2 == 0 else [0, 1, 2]
                    msl = m % 2
                    ev_d = {}
                    for k in range(NPAIR):
                        for ci, (c0, cn) in enumerate(chunks):
                            ev_d[ci] = P.op("pe", lambda t, k=k, msl=msl,
                                            c0=c0, cn=cn, ci=ci, bset=bset: t.matmul(
                                                pool[bset[ci]][0][:, 0:cn],
                                                wdn[msl][:, k * 128:(k + 1) * 128],
                                                inter[:, k * TS + c0: k * TS + c0 + cn],
                                                start=(k == 0), stop=(k == NPAIR - 1)),
                                            wait=(wdnrw[msl].rdep()
                                                  + (pool[bset[ci]][1].wdep()
                                                     + [ev_int_c[ci]] if k == 0
                                                     else [])) if k == 0
                                            else [],
                                            inc=(k == NPAIR - 1))
                    wdnrw[msl].read(ev_d[len(chunks) - 1])
                    if m + 2 < NDM:
                        wd_dma(e, m + 2)
                    for ci, (c0, cn) in enumerate(chunks):
                        pd, pdrw = pool[bset[ci]]
                        pdrw.wrote(ev_d[ci])
                        ev_x = P.op("ve", lambda v, m=m, c0=c0, cn=cn, pd=pd:
                                    v.tensor_add(
                                        x_sb[:, m * TS + c0: m * TS + c0 + cn],
                                        x_sb[:, m * TS + c0: m * TS + c0 + cn],
                                        pd[:, 0:cn]),
                                    wait=[ev_d[ci]], inc=True)
                        pdrw.read(ev_x)
                        ev_x_last = ev_x
            return ev_x_last

        with ExitStack() as ctx:
            emit_ffn(ctx, "f1", 0, OWN,
                     stage={"ln1": "ln", "router1": "router"}.get(upto, "full"))
            if "x1" in taps:
                P.barrier()
                tap("x1", x_sb, [128, NF * TS], [])
            P.barrier()

        if upto in ("attn", "all"):
            with ExitStack() as ctx:
                pool = alloc_psum_pool(ctx, "at")
                CE = ctx.enter_context
                msk = CE(nc.sbuf_tensor("msk_sb", [128, 3 * 256], F32))
                ctab = CE(nc.sbuf_tensor("ctab_sb", [128, TS], F32))
                stab = CE(nc.sbuf_tensor("stab_sb", [128, TS], F32))
                snk = CE(nc.sbuf_tensor("snk_sb", [128, H], F32))
                snko = CE(nc.sbuf_tensor("snko_sb", [128, H], F32))
                snkn = CE(nc.sbuf_tensor("snkn_sb", [128, H], F32))
                bq_sb = CE(nc.sbuf_tensor("bq_sb", [128, NMQ], F32))
                bk_sb = CE(nc.sbuf_tensor("bk_sb", [128, NMK], F32))
                bv_sb = CE(nc.sbuf_tensor("bv_sb", [128, 512], F32))
                bo_sb = CE(nc.sbuf_tensor("bo_sb", [128, NMQ], F32))
                swp_f = CE(nc.sbuf_tensor("swp_f", [128, 128], F32))
                lnT = CE(nc.sbuf_tensor("lnT2", [128, NF * TS], BF16))
                v_sb = CE(nc.sbuf_tensor("v_sb", [128, NTT * 512], BF16))
                krot = CE(nc.sbuf_tensor("krot", [128, NMK * EXT], BF16))
                qrot = CE(nc.sbuf_tensor("qrot", [128, NMQ * OWN], BF16))
                wblk = [CE(nc.sbuf_tensor(f"wblk{i}", [128, 2048], BF16))
                        for i in range(3)]
                wvb = [CE(nc.sbuf_tensor(f"wvb{i}", [128, 512], BF16))
                       for i in range(2)]
                wblkrw = [RW(f"wblk{i}") for i in range(3)]
                wvbrw = [RW("wvb0"), RW("wvb1")]
                hidx_sb = CE(nc.sbuf_tensor("hidx_sb", [128, 2], mybir.dt.int32))
                halo_sb = CE(nc.sbuf_tensor("halo_sb", [128, 2048], BF16))
                cc_sem = CE(nc.semaphore("sem_cc"))
                sems["cc"] = cc_sem
                ev_ld = None
                for dst, src in ((msk, d_msk), (ctab, d_cos), (stab, d_sin),
                                 (snk, d_snk), (snkn, d_snkn), (snko, d_snko),
                                 (bq_sb, d_bq),
                                 (bk_sb, d_bk), (bv_sb, d_bv), (bo_sb, d_bo),
                                 (swp_f, d_swp), (hidx_sb, d_hidx)):
                    ev_ld = P.op("sy", lambda s, a=dst, b=src: s.dma_start(
                        out=a[:], in_=b), inc=True)
                ev_lnT, ev_lnc2 = emit_ln(ctx, "l2", lnT, 0, OWN, pool)
                if "lnT2" in taps:
                    tap("lnT2", lnT, [128, NF * TS], [ev_lnT])

                with ExitStack() as c2:
                    C2 = c2.enter_context
                    kf = [C2(nc.sbuf_tensor(f"kf{i}", [128, 512], F32))
                          for i in range(2)]
                    kt1 = C2(nc.sbuf_tensor("kt1", [128, 512], F32))
                    kt2 = C2(nc.sbuf_tensor("kt2", [128, 512], F32))
                    kfrw = [RW("kf0"), RW("kf1")]
                    pctr = 0
                    ev_krot = None
                    for mb in range(NMK):
                        sl = mb % 3
                        ev_w = P.op("gp", lambda g, mb=mb, sl=sl: g.dma_start(
                            out=wblk[sl][:], in_=d_wk[mb]),
                            wait=wblkrw[sl].wdep(), inc=True)
                        wblkrw[sl].wrote(ev_w)
                        ev_mk_last = None
                        for (c0, cn) in chunk_list(0, OWN):
                            pp, pprw = pool[2 + pctr % 2]
                            ps, psrw = pool[4 + pctr % 2]
                            fsl = pctr % 2
                            pctr += 1
                            ev_m = None
                            for k in range(NF):
                                ev_m = P.op("pe", lambda t, k=k, sl=sl, c0=c0, cn=cn,
                                            pp=pp: t.matmul(
                                                pp[:, 0:cn],
                                                wblk[sl][:, k * 128:(k + 1) * 128],
                                                lnT[:, k * TS + c0:k * TS + c0 + cn],
                                                start=(k == 0), stop=(k == NF - 1)),
                                            wait=(wblkrw[sl].rdep() + pprw.wdep()
                                                  + [ev_lnc2.get(c0, ev_lnT)])
                                            if k == 0 else [],
                                            inc=(k == NF - 1))
                            pprw.wrote(ev_m)
                            ev_mk_last = ev_m
                            ev_f = P.op("se", lambda s, fsl=fsl, mb=mb, cn=cn, pp=pp:
                                        s.activation(kf[fsl][:, 0:cn], pp[:, 0:cn],
                                                     AF.Identity, bias=bk_sb[:, mb:mb + 1]),
                                        wait=[ev_m, ev_ld] + kfrw[fsl].wdep(), inc=True)
                            pprw.read(ev_f)
                            kfrw[fsl].wrote(ev_f)
                            ev_sw = P.op("pe", lambda t, fsl=fsl, cn=cn, ps=ps:
                                         t.matmul(ps[:, 0:cn], swp_f[:],
                                                  kf[fsl][:, 0:cn],
                                                  start=True, stop=True),
                                         wait=[ev_f] + psrw.wdep(), inc=True)
                            psrw.wrote(ev_sw)
                            P.op("ve", lambda v, c0=c0, cn=cn, ps=ps: v.tensor_mul(
                                kt1[:, 0:cn], stab[:, c0:c0 + cn],
                                ps[:, 0:cn]),
                                wait=[ev_sw, ev_ld])
                            ev_r1 = P.op("ve", lambda v, fsl=fsl, c0=c0, cn=cn:
                                         v.tensor_mul(kt2[:, 0:cn], kf[fsl][:, 0:cn],
                                                      ctab[:, c0:c0 + cn]),
                                         inc=True)
                            psrw.read(ev_r1)
                            kfrw[fsl].read(ev_r1)
                            ev_krot = P.op("ve", lambda v, mb=mb, c0=c0, cn=cn:
                                           v.tensor_add(
                                               krot[:, mb * EXT + HALO + c0:
                                                    mb * EXT + HALO + c0 + cn],
                                               kt1[:, 0:cn], kt2[:, 0:cn]), inc=True)
                        wblkrw[sl].read(ev_mk_last)

                    ev_vdone = None
                    for tp in range(OWN // 256):
                        t0, t1 = 2 * tp, 2 * tp + 1
                        pp0, pp0rw = pool[6]
                        pp1, pp1rw = pool[7]
                        ev_m0 = ev_m1 = None
                        for k in range(NF):
                            sl = k % 2
                            ev_w = P.op("gp", lambda g, k=k, sl=sl: g.dma_start(
                                out=wvb[sl][:], in_=d_wv[k * 128:(k + 1) * 128, :]),
                                wait=wvbrw[sl].wdep(), inc=True)
                            wvbrw[sl].wrote(ev_w)
                            ev_m0 = P.op("pe", lambda t, k=k, sl=sl, t0=t0: t.matmul(
                                pp0[:, 0:512],
                                lnT[:, k * TS + t0 * 128: k * TS + (t0 + 1) * 128],
                                wvb[sl][:], start=(k == 0), stop=(k == NF - 1)),
                                wait=[ev_w] + (pp0rw.wdep() if k == 0 else []),
                                inc=(k == NF - 1))
                            ev_m1 = P.op("pe", lambda t, k=k, sl=sl, t1=t1: t.matmul(
                                pp1[:, 0:512],
                                lnT[:, k * TS + t1 * 128: k * TS + (t1 + 1) * 128],
                                wvb[sl][:], start=(k == 0), stop=(k == NF - 1)),
                                wait=(pp1rw.wdep() if k == 0 else []),
                                inc=True)
                            wvbrw[sl].read(ev_m1)
                        pp0rw.wrote(ev_m0)
                        pp1rw.wrote(ev_m1)
                        ev_v0 = P.op("ve", lambda v, t0=t0, pp0=pp0: v.tensor_add(
                            v_sb[:, (t0 + 1) * 512:(t0 + 2) * 512], bv_sb[:],
                            pp0[:, 0:512]),
                            wait=[ev_m0, ev_ld], inc=True)
                        pp0rw.read(ev_v0)
                        ev_vdone = P.op("ve", lambda v, t1=t1, pp1=pp1: v.tensor_add(
                            v_sb[:, (t1 + 1) * 512:(t1 + 2) * 512], bv_sb[:],
                            pp1[:, 0:512]),
                            wait=[ev_m1], inc=True)
                        pp1rw.read(ev_vdone)
                with ExitStack() as c2:
                    C2 = c2.enter_context
                    qf = [C2(nc.sbuf_tensor(f"qf{i}", [128, 512], F32))
                          for i in range(2)]
                    qt1 = C2(nc.sbuf_tensor("qt1", [128, 512], F32))
                    qt2 = C2(nc.sbuf_tensor("qt2", [128, 512], F32))
                    qfrw = [RW("qf0"), RW("qf1")]
                    ev_qrot = None
                    for m in range(NMQ):
                        wsl = m % 3
                        ev_w = P.op("gp", lambda gg, m=m, wsl=wsl: gg.dma_start(
                            out=wblk[wsl][:], in_=d_wq[m]),
                            wait=wblkrw[wsl].wdep(), inc=True)
                        wblkrw[wsl].wrote(ev_w)
                        ev_m = None
                        for grp in range(2):
                            g0 = grp * 512
                            fsl = grp % 2
                            pp, pprw = pool[grp % 2]
                            ps, psrw = pool[2 + grp % 2]
                            for k in range(NF):
                                ev_m = P.op("pe", lambda t, k=k, wsl=wsl, g0=g0,
                                            pp=pp: t.matmul(
                                                pp[:, 0:512],
                                                wblk[wsl][:, k * 128:(k + 1) * 128],
                                                lnT[:, k * TS + g0:k * TS + g0 + 512],
                                                start=(k == 0), stop=(k == NF - 1)),
                                            wait=(wblkrw[wsl].rdep() + pprw.wdep())
                                            if k == 0 else [],
                                            inc=(k == NF - 1))
                            pprw.wrote(ev_m)
                            ev_f = P.op("se", lambda s, fsl=fsl, m=m, pp=pp:
                                        s.activation(qf[fsl][:], pp[:, 0:512],
                                                     AF.Identity,
                                                     bias=bq_sb[:, m:m + 1]),
                                        wait=[ev_m] + qfrw[fsl].wdep(), inc=True)
                            pprw.read(ev_f)
                            qfrw[fsl].wrote(ev_f)
                            ev_sw = P.op("pe", lambda t, fsl=fsl, ps=ps: t.matmul(
                                ps[:, 0:512], swp_f[:], qf[fsl][:],
                                start=True, stop=True),
                                wait=[ev_f] + psrw.wdep(), inc=True)
                            psrw.wrote(ev_sw)
                            P.op("ve", lambda v, g0=g0, ps=ps: v.tensor_mul(
                                qt1[:], stab[:, g0:g0 + 512], ps[:, 0:512]),
                                wait=[ev_sw])
                            ev_r1 = P.op("ve", lambda v, fsl=fsl, g0=g0: v.tensor_mul(
                                qt2[:], qf[fsl][:], ctab[:, g0:g0 + 512]), inc=True)
                            psrw.read(ev_r1)
                            qfrw[fsl].read(ev_r1)
                            ev_qrot = P.op("ve", lambda v, m=m, g0=g0: v.tensor_add(
                                qrot[:, m * OWN + g0:m * OWN + g0 + 512],
                                qt1[:], qt2[:]), inc=True)
                        wblkrw[wsl].read(ev_m)

                ev_kv_local = max(ev_krot, ev_vdone,
                                  key=lambda e: e[1] if e else 0)
                ev_pk = None
                for edge, tok0 in ((0, 0), (1, OWN - 128)):
                    for mb in range(NMK):
                        ev_pk = P.op("sy", lambda ss, edge=edge, tok0=tok0, mb=mb:
                                     ss.dma_start(
                                         out=d_ccin[edge, :, mb * 128:(mb + 1) * 128],
                                         in_=krot[:, mb * EXT + HALO + tok0:
                                                  mb * EXT + HALO + tok0 + 128]),
                                     wait=[ev_krot], inc=True)
                    ev_pk = P.op("sy", lambda ss, edge=edge, tok0=tok0:
                                 ss.dma_start(
                                     out=d_ccin[edge, :, 512:1024],
                                     in_=v_sb[:, (1 + tok0 // 128) * 512:
                                              (2 + tok0 // 128) * 512]),
                                 wait=[ev_vdone], inc=True)
                P.op("gp", lambda g: g.collective_compute(
                    "AllGather", mybir.AluOpType.bypass,
                    replica_groups=[list(range(NCORE))],
                    ins=[d_ccin.opt()], outs=[d_ccout.opt()]).then_inc(cc_sem),
                    wait=[ev_pk])
                ev_g0 = P.op("gp", lambda g: g.indirect_dma_start(
                    halo_sb[:, 0:1024], None,
                    d_ccout.rearrange("r p c -> (r p) c"),
                    IndirectOffsetOnAxis(hidx_sb[:, 0:1], 0)),
                    wait=[("cc", 1), ev_ld], inc=True)
                ev_g1 = P.op("gp", lambda g: g.indirect_dma_start(
                    halo_sb[:, 1024:2048], None,
                    d_ccout.rearrange("r p c -> (r p) c"),
                    IndirectOffsetOnAxis(hidx_sb[:, 1:2], 0)), inc=True)
                for mb in range(NMK):
                    P.op("ve", lambda v, mb=mb: v.tensor_copy(
                        krot[:, mb * EXT: mb * EXT + 128],
                        halo_sb[:, mb * 128:(mb + 1) * 128]),
                        wait=[ev_g0] if mb == 0 else [])
                    P.op("ve", lambda v, mb=mb: v.tensor_copy(
                        krot[:, mb * EXT + HALO + OWN: (mb + 1) * EXT],
                        halo_sb[:, 1024 + mb * 128: 1024 + (mb + 1) * 128]),
                        wait=[ev_g1] if mb == 0 else [])
                P.op("ve", lambda v: v.tensor_copy(
                    v_sb[:, 0:512], halo_sb[:, 512:1024]))
                ev_halo = P.op("ve", lambda v: v.tensor_copy(
                    v_sb[:, (NTT - 1) * 512: NTT * 512],
                    halo_sb[:, 1536:2048]), inc=True)
                

                if "krot" in taps:
                    tap("krot", krot, [128, NMK * EXT], [ev_krot])
                if "v_sb" in taps:
                    tap("v_sb", v_sb, [128, NTT * 512], [ev_vdone])
                P.barrier()

                with ExitStack() as c2:
                    C2 = c2.enter_context
                    aout = C2(nc.sbuf_tensor("aout", [128, NMQ * QG], BF16))
                    pbuf = [C2(nc.sbuf_tensor(f"pbuf{i}", [128, KW], BF16))
                            for i in range(8)]
                    ptb = [C2(nc.sbuf_tensor(f"ptb{i}", [128, KW], BF16))
                           for i in range(4)]
                    nm4 = C2(nc.sbuf_tensor("anm4", [128, 4], F32))
                    ds4 = C2(nc.sbuf_tensor("ads4", [128, 4], F32))
                    es4 = C2(nc.sbuf_tensor("aes4", [128, 4], F32))
                    diag = [C2(nc.sbuf_tensor(f"adg{i}", [128, 128], BF16))
                            for i in range(8)]
                    pbrw = [RW(f"pb{i}") for i in range(8)]
                    ptrw = [RW(f"pt{i}") for i in range(4)]
                    dgrw = [RW(f"dg{i}") for i in range(8)]
                    for g in (1, 2, 0, 3):
                        g0 = g * QG
                        pav, pavrw = pool[3]
                        ev_ao = None
                        SKEW = 5
                        st_a = {}
                        st_b = {}
                        quad_prev = [None, None]
                        exp_prev = [None] * 4

                        def stage_a(idx):
                            qq, hh = idx // H, idx % H
                            qt = 2 * g + qq
                            mvar = 1 if qt == 0 else (2 if qt == NTT - 3 else 0)
                            ph, sub = hh // 2, hh % 2
                            gg_, i_ = ph // 4, ph % 4
                            h = gg_ * 8 + sub * 4 + i_
                            kvg = 2 * gg_ + sub
                            rbase = sub * 64
                            sl = idx % 4
                            slb = idx % 8
                            psc, pscrw = pool[4 + sl]
                            ev_sc = P.op("pe", lambda t, ph=ph, rbase=rbase,
                                         kvg=kvg, qt=qt, psc=psc: t.matmul(
                                             psc[:, 0:KW],
                                             qrot[rbase:rbase + 64,
                                                  ph * OWN + qt * 128:
                                                  ph * OWN + qt * 128 + 128],
                                             krot[rbase:rbase + 64,
                                                  (kvg // 2) * EXT + qt * 128:
                                                  (kvg // 2) * EXT + qt * 128 + KW],
                                             start=True, stop=True),
                                         wait=[ev_qrot,
                                               ev_halo if qt in (0, NTT - 3)
                                               else ev_kv_local]
                                         + pscrw.wdep(),
                                         inc=True)
                            pscrw.wrote(ev_sc)
                            ev_mk = P.op("ve", lambda v, mvar=mvar, psc=psc:
                                         v.tensor_tensor(
                                             psc.ap().rearrange(
                                                 "p (b c) -> p b c", c=128)[:, 0:3:2, :],
                                             msk.ap().rearrange(
                                                 "p (v b c) -> p v b c", v=3, c=128)
                                             [:, mvar, :, :],
                                             psc.ap().rearrange(
                                                 "p (b c) -> p b c", c=128)[:, 0:3:2, :],
                                             ALU.add),
                                         wait=[ev_sc, ev_ld], inc=True)
                            pscrw.wrote(ev_mk)
                            ev_rm = P.op("ve", lambda v, sl=sl, psc=psc: v.reduce_max(
                                nm4[:, sl:sl + 1], psc[:, 0:KW],
                                axis=AX.X, negate=True),
                                wait=[ev_mk, exp_prev[sl]], inc=True)
                            pscrw.read(ev_rm)
                            ev_ex = P.op("se", lambda s, sl=sl, slb=slb, psc=psc:
                                         s.activation(
                                             pbuf[slb][:], psc[:, 0:KW], AF.Exp,
                                             bias=nm4[:, sl:sl + 1],
                                             accum_out=ds4[:, sl:sl + 1]),
                                         wait=[ev_rm, quad_prev[0]]
                                         + pbrw[slb].wdep(), inc=True)
                            pscrw.read(ev_ex)
                            pbrw[slb].wrote(ev_ex)
                            exp_prev[sl] = ev_ex
                            dg = idx % 8
                            st_a[idx] = (ev_ex, kvg, qt, slb, dg, ph, rbase)
                            if sl == 3:
                                hh0 = hh - 3
                                ev_nm4 = P.op("ve", lambda v, hh0=hh0: v.tensor_add(
                                    es4[:], snko[:, hh0:hh0 + 4], nm4[:]),
                                    wait=[ev_rm, quad_prev[1]], inc=True)
                                ev_es4 = P.op("se", lambda s: s.activation(
                                    es4[:], es4[:], AF.Exp),
                                    wait=[ev_nm4, ev_ex], inc=True)
                                P.op("ve", lambda v: v.tensor_add(
                                    ds4[:], ds4[:], es4[:]), wait=[ev_es4])
                                ev_rc4 = P.op("ve", lambda v: v.reciprocal(
                                    ds4[:], ds4[:]), inc=True)
                                ev_dgl = None
                                for j in range(4):
                                    dgj = (idx - 3 + j) % 8
                                    ev_dgl = P.op("ve", lambda v, j=j, dgj=dgj:
                                                  v.tensor_scalar_mul(
                                                      diag[dgj][:], id_f[:],
                                                      ds4[:, j:j + 1]),
                                                  wait=[ev_rc4] + dgrw[dgj].wdep(),
                                                  inc=True)
                                    dgrw[dgj].wrote(ev_dgl)
                                    st_a[idx - 3 + j] += (ev_dgl,)
                                quad_prev[0] = ev_dgl
                                quad_prev[1] = ev_es4

                        def stage_b(idx):
                            ev_ex, kvg, qt, slb, dg, ph, rbase, ev_dg = st_a.pop(idx)
                            sl = idx % 4
                            ptp, ptprw = pool[idx % 2]
                            ev_tp = None
                            for jj in range(3):
                                ev_tp = P.op("pe", lambda t, slb=slb, jj=jj, ptp=ptp,
                                             dg=dg: t.matmul(
                                                 ptp[:, jj * 128:(jj + 1) * 128],
                                                 pbuf[slb][:, jj * 128:(jj + 1) * 128],
                                                 diag[dg][:], start=True, stop=True),
                                             wait=([ev_ex, ev_dg] + ptprw.wdep())
                                             if jj == 0 else [],
                                             inc=(jj == 2))
                            dgrw[dg].read(ev_tp)
                            ptprw.wrote(ev_tp)
                            pbrw[slb].read(ev_tp)
                            ev_pt = P.op("se", lambda ss, sl=sl, ptp=ptp:
                                         ss.copy(ptb[sl][:], ptp[:, 0:KW]),
                                         wait=[ev_tp] + ptrw[sl].wdep(), inc=True)
                            ptprw.read(ev_pt)
                            ptrw[sl].wrote(ev_pt)
                            st_b[idx] = (ev_pt, kvg, qt, sl, ph, rbase)

                        def stage_c(idx):
                            nonlocal ev_ao
                            ev_pt, kvg, qt, sl, ph, rbase = st_b.pop(idx)
                            qq = idx // H
                            rb = rbase
                            ev_av = None
                            for jj in range(3):
                                ev_av = P.op("pe", lambda t, jj=jj, qt=qt, kvg=kvg,
                                             sl=sl, rb=rb: t.matmul(
                                                 pav[rb:rb + 64, 0:128],
                                                 v_sb[:, (qt + jj) * 512 + kvg * 64:
                                                      (qt + jj) * 512 + kvg * 64 + 64],
                                                 ptb[sl][:, jj * 128:(jj + 1) * 128],
                                                 start=(jj == 0), stop=(jj == 2)),
                                             wait=([ev_pt,
                                                    ev_halo if qt in (0, NTT - 3)
                                                    else ev_kv_local]
                                                   + (pavrw.wdep() if rb == 0
                                                      else []))
                                             if jj == 0 else [],
                                             inc=(jj == 2))
                            ptrw[sl].read(ev_av)
                            if rb == 0:
                                pavrw.wrote(ev_av)
                            else:
                                ev_ao = P.op("se", lambda ss, ph=ph, qq=qq, pav=pav:
                                             ss.copy(
                                                 aout[:, ph * QG + qq * 128:
                                                      ph * QG + qq * 128 + 128],
                                                 pav[:, 0:128]),
                                             wait=[ev_av], inc=True)
                                pavrw.read(ev_ao)

                        SKEW2 = 3
                        for idx in range(2 * H + SKEW + SKEW2):
                            if idx < 2 * H:
                                stage_a(idx)
                            if SKEW <= idx < 2 * H + SKEW:
                                stage_b(idx - SKEW)
                            if idx >= SKEW + SKEW2:
                                stage_c(idx - SKEW - SKEW2)
                        ev_aout = ev_ao
                        if "aout" in taps and g == 0:
                            tap("aout", aout, [128, NMQ * QG], [ev_aout])
                        for mf in range(NMQ):
                            sl = mf % 3
                            ev_w = P.op("gp", lambda gg, mf=mf, sl=sl: gg.dma_start(
                                out=wblk[sl][:], in_=d_wo[mf]),
                                wait=wblkrw[sl].wdep(), inc=True)
                            wblkrw[sl].wrote(ev_w)
                            pp, pprw = pool[2 + mf % 2]
                            ev_m = None
                            for k in range(NMQ):
                                ev_m = P.op("pe", lambda t, k=k, sl=sl, pp=pp:
                                            t.matmul(
                                                pp[:, 0:QG],
                                                wblk[sl][:, k * 128:(k + 1) * 128],
                                                aout[:, k * QG:(k + 1) * QG],
                                                start=(k == 0), stop=(k == NMQ - 1)),
                                            wait=(wblkrw[sl].rdep() + pprw.wdep()
                                                  + [ev_aout]) if k == 0 else [],
                                            inc=(k == NMQ - 1))
                            pprw.wrote(ev_m)
                            wblkrw[sl].read(ev_m)
                            ev_x = P.op("ve", lambda v, mf=mf, g0=g0, pp=pp:
                                        v.scalar_tensor_tensor(
                                            x_sb[:, mf * TS + g0: mf * TS + g0 + QG],
                                            pp[:, 0:QG], bo_sb[:, mf:mf + 1],
                                            x_sb[:, mf * TS + g0: mf * TS + g0 + QG],
                                            op0=ALU.add, op1=ALU.add),
                                        wait=[ev_m], inc=True)
                            pprw.read(ev_x)
                if "x2" in taps:
                    P.barrier()
                    tap("x2", x_sb, [128, NF * TS], [])
                P.barrier()

        if upto == "all":
            with ExitStack() as ctx:
                emit_ffn(ctx, "f2", 0, OWN)
                P.barrier()

        with ExitStack() as ctx:
            pool = alloc_psum_pool(ctx, "out")
            ob = [ctx.enter_context(nc.sbuf_tensor(f"ob{i}", [128, DM], F32))
                  for i in range(2)]
            obrw = [RW("ob0"), RW("ob1")]
            ev_st = None
            for qt in range(8):
                sl = qt % 2
                ev_e = None
                for f in range(NF):
                    pp, pprw = pool[f % 4]
                    ev_t = P.op("pe", lambda t, f=f, qt=qt, pp=pp: t.transpose(
                        pp[:, 0:128],
                        x_sb[:, f * TS + qt * 128:
                             f * TS + (qt + 1) * 128],
                        id_f[:]),
                        wait=pprw.wdep(), inc=True)
                    pprw.wrote(ev_t)
                    ev_e = P.op("ve", lambda v, f=f, sl=sl, pp=pp: v.tensor_copy(
                        ob[sl][:, f * 128:(f + 1) * 128], pp[:, 0:128]),
                        wait=[ev_t] + (obrw[sl].wdep() if f == 0 else []), inc=True)
                    pprw.read(ev_e)
                obrw[sl].wrote(ev_e)
                ev_st = P.op("sy", lambda s, qt=qt, sl=sl: s.dma_start(
                    out=d_y[qt * 128:(qt + 1) * 128, :], in_=ob[sl][:]),
                    wait=[ev_e], inc=True)
                obrw[sl].read(ev_st)
            P.op("ve", None, wait=[ev_st])

        with nc.Block() as block:
            @block.tensor
            def _(t):
                P.emit(t, "pe", sems)

            @block.vector
            def _(v):
                P.emit(v, "ve", sems)

            @block.scalar
            def _(s):
                P.emit(s, "se", sems)

            @block.sync
            def _(s):
                P.emit(s, "sy", sems)

            @block.gpsimd
            def _(g):
                P.emit(g, "gp", sems)

    return nc, d_taps


def _retile_lhsT(W):
    K, M = W.shape
    a = W.reshape(K // 128, 128, M // 128, 128).transpose(2, 1, 0, 3)
    return np.ascontiguousarray(a.reshape(M // 128, 128, K))


def prep_inputs(L0_attn_hidden_states, L0_attn_Wq, L0_attn_bq, L0_attn_Wk,
                L0_attn_bk, L0_attn_Wv, L0_attn_bv, L0_attn_Wo, L0_attn_bo,
                L0_attn_sinks, L0_ffn_gate_w, L0_ffn_gate_b, L0_ffn_expert_bias,
                L0_ffn_w_gate_up, L0_ffn_w_down):
    f32 = np.float32
    x = np.asarray(L0_attn_hidden_states, f32).reshape(S, DM)
    xT = np.ascontiguousarray(x.T)

    wgu = np.asarray(L0_ffn_w_gate_up, f32)
    a = wgu.reshape(NE, 2, NPAIR, 128, NF, 128).transpose(0, 2, 1, 5, 4, 3)
    arr_gu = np.ascontiguousarray(a.reshape(NE, GUBLK, 128, 2048))

    wdn = np.asarray(L0_ffn_w_down, f32)
    arr_wd = np.ascontiguousarray(
        wdn.reshape(NE, NDM, 128, NPAIR, 128).transpose(0, 1, 4, 3, 2)
        .reshape(NE, NDM, 128, 2816))

    hd_perm = np.empty(H * D, dtype=np.int64)
    for h in range(H):
        tile_ = (h // 8) * 4 + (h % 4)
        base_ = ((h // 4) % 2) * 64
        hd_perm[h * D:(h + 1) * D] = tile_ * 128 + base_ + np.arange(D)
    wq_flat = np.asarray(L0_attn_Wq, f32).reshape(DM, H * D)
    wq_p = np.empty_like(wq_flat)
    wq_p[:, hd_perm] = wq_flat
    wo_flat = np.asarray(L0_attn_Wo, f32).reshape(H * D, DM)
    wo_p = np.empty_like(wo_flat)
    wo_p[hd_perm, :] = wo_flat
    wq_a = _retile_lhsT(wq_p)
    wk_a = _retile_lhsT(np.asarray(L0_attn_Wk, f32).reshape(DM, HKV * D))
    wo_a = _retile_lhsT(wo_p)
    wv_n = np.ascontiguousarray(np.asarray(L0_attn_Wv, f32).reshape(DM, HKV * D))

    gw = np.asarray(L0_ffn_gate_w, f32)
    gw_a = np.ascontiguousarray(gw.T.reshape(NF, 128, NE).transpose(1, 0, 2)
                                .reshape(128, NF * NE))

    def tile_col(b):
        return np.ascontiguousarray(np.asarray(b, f32).reshape(-1, 128).T)

    bq_flat = np.asarray(L0_attn_bq, f32).reshape(H * D)
    bq_p = np.empty_like(bq_flat)
    bq_p[hd_perm] = bq_flat
    bq_t = tile_col(bq_p)
    bk_t = tile_col(np.asarray(L0_attn_bk, f32).reshape(HKV * D))
    bo_t = tile_col(np.asarray(L0_attn_bo, f32).reshape(DM))
    bv_bc = np.ascontiguousarray(np.broadcast_to(
        np.asarray(L0_attn_bv, f32).reshape(1, HKV * D), (128, HKV * D)))
    gb_bc = np.ascontiguousarray(np.broadcast_to(
        np.asarray(L0_ffn_gate_b, f32).reshape(1, NE), (128, NE)))
    eb_bc = np.ascontiguousarray(np.broadcast_to(
        np.asarray(L0_ffn_expert_bias, f32).reshape(1, NE), (128, NE)))
    snk = np.ascontiguousarray(np.broadcast_to(
        np.asarray(L0_attn_sinks, f32).reshape(1, H), (128, H)))
    snkn = np.ascontiguousarray(-snk)
    hh_order = []
    for hh in range(H):
        ph, sub = hh // 2, hh % 2
        hh_order.append((ph // 4) * 8 + sub * 4 + (ph % 4))
    snko = np.ascontiguousarray(snk[:, hh_order])

    id_f = np.eye(128, dtype=f32)
    sel = np.zeros((NE, NE * 128), dtype=f32)
    for e in range(NE):
        sel[e, e * 128:(e + 1) * 128] = 1.0
    swp = np.zeros((128, 128), dtype=f32)
    for i in range(0, 128, 2):
        swp[i, i + 1] = 1.0
        swp[i + 1, i] = 1.0

    inv_freq = (1.0 / (WL ** (np.arange(D // 2, dtype=f32) * 2.0 / D))).astype(f32)
    per_core = []
    for c in range(NCORE):
        lo = c * OWN - HALO
        xT_own = np.ascontiguousarray(xT[:, c * OWN:(c + 1) * OWN])

        gpos = (c * OWN + np.arange(OWN)).astype(f32)
        ang = gpos[None, :] * inv_freq[:, None]
        cos32 = np.cos(ang).astype(f32) * RSCALE
        sin32 = np.sin(ang).astype(f32) * RSCALE
        dd = np.arange(128)
        ii = (dd % 64) // 2
        sgn = np.where(dd % 2 == 0, -1.0, 1.0).astype(f32)
        ctab = np.ascontiguousarray(cos32[ii])
        stab = np.ascontiguousarray(sin32[ii] * sgn[:, None])

        qi = np.arange(128)[:, None]
        j = np.arange(KW)[None, :]
        band = (j - qi >= 0) & (j - qi <= 2 * SW)
        v0 = np.where(band, 0.0, NEG).astype(f32)
        if c == 0:
            left_ok = np.broadcast_to((lo + j) >= 0, band.shape)
            v1 = np.where(band & left_ok, 0.0, NEG).astype(f32)
        else:
            v1 = v0
        if c == NCORE - 1:
            right_ok = np.broadcast_to((lo + (NTT - 3) * 128 + j) < S, band.shape)
            v2 = np.where(band & right_ok, 0.0, NEG).astype(f32)
        else:
            v2 = v0
        msk = np.ascontiguousarray(np.concatenate(
            [np.concatenate([v[:, 0:128], v[:, 256:KW]], axis=1)
             for v in (v0, v1, v2)], axis=1))
        pp_ = np.arange(128, dtype=np.int32)
        left_slab = ((c - 1) % NCORE) * 2 + 1
        right_slab = ((c + 1) % NCORE) * 2 + 0
        hidx = np.stack([left_slab * 128 + pp_, right_slab * 128 + pp_],
                        axis=1).astype(np.int32)
        per_core.append({"xT": xT_own, "ctab": ctab, "stab": stab, "msk": msk,
                         "hidx": hidx})

    shared = {
        "wgu": arr_gu, "wdn": arr_wd, "wq": wq_a, "wk": wk_a, "wv": wv_n,
        "wo": wo_a, "gw": gw_a, "bq": bq_t, "bk": bk_t, "bv": bv_bc, "bo": bo_t,
        "gb": gb_bc, "eb": eb_bc, "snk": snk, "snkn": snkn, "snko": snko, "idf": id_f,
        "swp": swp, "sel": sel,
    }
    in_maps = []
    for c in range(NCORE):
        m = dict(shared)
        m.update(per_core[c])
        in_maps.append(m)
    return in_maps


def kernel(**inputs):
    in_maps = prep_inputs(**inputs)
    nc, _ = build_core_program(upto="all", taps=())
    res = run_bass_kernel_spmd(nc, in_maps, core_ids=list(range(NCORE)))
    y = np.concatenate([r["y"] for r in res.results], axis=0)
    return y.reshape(1, S, DM)
